# version 12
# baseline (speedup 1.0000x reference)
import math
from collections import defaultdict
import numpy as np
import ml_dtypes
import concourse.bass as bass
import concourse.mybir as mybir

F32 = mybir.dt.float32
BF16 = mybir.dt.bfloat16
AF = mybir.ActivationFunctionType
ALU = mybir.AluOpType
AX = mybir.AxisListType
BF = ml_dtypes.bfloat16
KB = 1024


class Cfg:
    def __init__(self, T=2048, NSEQ=4):
        assert T % 512 == 0
        self.T = T; self.NSEQ = NSEQ
        self.D = 1024; self.H = 8; self.HD = 128; self.CH = 8
        self.NT = T // 128; self.NQB = T // 512; self.NG = T // 32
        self.SCALE = 1.0 / math.sqrt(128.0)
        self.CSHIFT = 12.0


def build_nc(cfg):
    T, NSEQ, H, CH, NT, NQB, NG = (cfg.T, cfg.NSEQ, cfg.H, cfg.CH, cfg.NT,
                                   cfg.NQB, cfg.NG)
    nc = bass.Bass("TRN2", target_bir_lowering=False)

    dx = nc.dram_tensor("x", [NSEQ, T, cfg.D], F32, kind="ExternalInput").ap()
    dxT = nc.dram_tensor("xT", [NSEQ, cfg.D, T], F32, kind="ExternalInput").ap()
    dwq = nc.dram_tensor("wq", [128, CH * 1024], F32, kind="ExternalInput").ap()
    dwkv = nc.dram_tensor("wkv", [128, CH * 256], F32, kind="ExternalInput").ap()
    dwo = nc.dram_tensor("wo", [128, CH * 1024], F32, kind="ExternalInput").ap()
    dwc = nc.dram_tensor("wc", [128, CH * 128], F32, kind="ExternalInput").ap()
    dw1 = nc.dram_tensor("w1", [4, 128, 2 * 128 * 32], F32, kind="ExternalInput").ap()
    dw2 = nc.dram_tensor("w2t", [4, 128, 32 * 256], F32, kind="ExternalInput").ap()
    dbq = nc.dram_tensor("bq", [128, CH], F32, kind="ExternalInput").ap()
    dbk = nc.dram_tensor("bk", [128, 1], F32, kind="ExternalInput").ap()
    dbvb = nc.dram_tensor("bvb", [128, 128], F32, kind="ExternalInput").ap()
    dbc = nc.dram_tensor("bc", [128, 1], F32, kind="ExternalInput").ap()
    db1 = nc.dram_tensor("b1", [128, 2048], F32, kind="ExternalInput").ap()
    db2 = nc.dram_tensor("b2", [128, 1024], F32, kind="ExternalInput").ap()
    dpehi = nc.dram_tensor("pehi", [128, T], F32, kind="ExternalInput").ap()
    dpelo = nc.dram_tensor("pelo", [128, T], F32, kind="ExternalInput").ap()
    dtie = nc.dram_tensor("tie", [1, T], BF16, kind="ExternalInput").ap()
    dones1 = nc.dram_tensor("ones1", [1, 128], BF16, kind="ExternalInput").ap()
    dident = nc.dram_tensor("ident", [128, 128], BF16, kind="ExternalInput").ap()
    didentf = nc.dram_tensor("identf", [128, 128], F32, kind="ExternalInput").ap()
    dzl = nc.dram_tensor("zl", [128, 896], BF16, kind="ExternalInput").ap()
    dy = nc.dram_tensor("y", [NSEQ, T, cfg.D], F32, kind="ExternalOutput").ap()
    drp = nc.dram_tensor("rp_scratch", [128, 64 * 256], BF16, kind="Internal").ap()
    cw1 = nc.dram_tensor("cw1", [4, 128, 2 * 128 * 32], BF16, kind="Internal").ap()
    cw2 = nc.dram_tensor("cw2", [4, 128, 32 * 256], BF16, kind="Internal").ap()
    cwo = nc.dram_tensor("cwo", [128, CH * 1024], BF16, kind="Internal").ap()
    cwkv = nc.dram_tensor("cwkv", [128, CH * 256], BF16, kind="Internal").ap()
    ctie = nc.dram_tensor("ctie", [1, T], BF16, kind="Internal").ap()

    def at(name, shape, dtype, off):
        return nc.alloc_sbuf_tensor_at(name, shape, dtype, offset=off).ap()

    o = [16704]
    def bump(n):
        r = o[0]; o[0] = (o[0] + n + 31) & ~31
        return r

    c_pe = at("c_pe", [128, T], F32, bump(T * 4))
    c_ident = at("c_ident", [128, 128], BF16, bump(256))
    c_ident_f = at("c_ident_f", [128, 128], F32, bump(512))
    c_zl = at("c_zl", [128, 896], BF16, bump(896 * 2))
    c_bq = at("c_bq", [128, CH], F32, bump(CH * 4))
    c_bk = at("c_bk", [128, 1], F32, bump(4))
    c_bvb = at("c_bvb", [128, 128], F32, bump(512))
    c_bc = at("c_bc", [128, 1], F32, bump(4))
    c_b1 = at("c_b1", [128, 2048], BF16, bump(2048 * 2))
    c_b2 = at("c_b2", [128, 1024], BF16, bump(2048))
    c_ones1 = at("c_ones1", [1, 128], BF16, bump(256))
    c_eps = at("c_eps", [128, 1], F32, bump(4))
    c_negC = at("c_negC", [128, 1], F32, bump(4))
    w_c = at("w_c", [128, CH * 128], BF16, bump(CH * 128 * 2))
    st_m = at("st_m", [128, NT], F32, bump(NT * 4))
    st_v = at("st_v", [128, NT], F32, bump(NT * 4))
    st_r = at("st_r", [128, NT], F32, bump(NT * 4))
    st_b = at("st_b", [128, NT], F32, bump(NT * 4))
    rs = at("rs", [128, 4], F32, bump(16))
    gmax = at("gmax", [128, NG], F32, bump(NG * 4))

    w_q = at("w_q", [128, CH * 1024], BF16, bump(CH * 1024 * 2))
    r1 = bump(NT * 1024 * 4)
    x2 = at("x2", [128, NT * 1024], F32, r1)

    sz_r2 = max(CH * T * 2, CH * 1024 * 2, 2 * NG * 128 * 2, NG * 256 * 2)
    r2 = bump(sz_r2)
    xT = at("xT", [128, CH * T], BF16, r2)
    x2t_stage = at("x2t", [128, 8 * 512], BF16, r2 + 20 * KB)
    c_tie = at("c_tie", [1, T], BF16, r2 + 16384)
    w_o = at("w_o", [128, CH * 1024], BF16, r2)
    gq = at("gq", [128, 2 * NG * 128], BF16, r2)
    rpe = at("rpe", [128, NG * 256], BF16, r2)

    sz_r3 = max(CH * T * 2, 2 * (2 * 128 * 32 * 2), 64 * 256 * 2)
    r3 = bump(sz_r3)
    qT = at("qT", [128, CH * T], BF16, r3)
    w_1 = [at(f"w_1_{i}", [128, 2 * 128 * 32], BF16, r3 + i * 2 * 128 * 32 * 2)
           for i in range(2)]

    rpn = at("rpn", [128, 64 * 256], BF16, r3)

    sz_attn = T * 2 + NT * 132 * 2 + 3 * 1024 + H * 512 * 2
    sz_r4 = max(sz_attn, 32 * 256 * 2)
    r4 = bump(sz_r4)
    kT = at("kT", [128, T], BF16, r4)
    vS = at("vS", [128, NT * 132], BF16, r4 + T * 2)
    pTp = at("pTp", [128, 1024], BF16, r4 + T * 2 + NT * 132 * 2)
    pT = [pTp[:, 0:512], pTp[:, 512:1024]]
    pTn = at("pTn", [128, 512], BF16, r4 + T * 2 + NT * 132 * 2 + 2 * 1024)
    atT = at("atT", [128, H * 512], BF16, r4 + T * 2 + NT * 132 * 2 + 3 * 1024)
    w_2t = [at("w_2t_0", [128, 32 * 256], BF16, r4)] * 2

    r5 = bump(max(2 * 1024 * 4, 2048 * 4))
    xtile = [at(f"xtile{i}", [128, 1024], F32, r5 + i * 4096) for i in range(2)]
    w_kv = at("w_kv", [128, CH * 256], BF16, r5)
    e1p = at("e1p", [128, 2048], F32, r5)

    r6 = bump(max(T * 4, 2048 * 2 + 2 * 512))
    lg = at("lg", [128, T], F32, r6)
    inr = at("inr", [128, 2048], BF16, r6)
    x2b = [at(f"x2b{i}", [128, 256], BF16, r6 + 2048 * 2 + i * 512) for i in range(2)]

    r7 = bump(T * 2 + NT * 128 * 2)
    pmT = at("pmT", [128, T], BF16, r7)
    pmt = at("pmt", [128, NT * 128], BF16, r7 + T * 2)
    casttile = at("casttile", [128, 4096], BF16, r7)

    assert o[0] <= 224 * KB - 256, f"SBUF over: {o[0]}"

    pp_sp = nc.alloc_psum_tensor("pp_sp", [128, 1024], F32).ap()
    pp_s = [pp_sp[:, 0:512], pp_sp[:, 512:1024]]
    pp_pv = [nc.alloc_psum_tensor(f"pp_pv{i}", [128, 512], F32).ap() for i in range(2)]
    pp_trf0 = nc.alloc_psum_tensor("pp_trf0", [128, 512], F32).ap()
    pp_trf = [pp_trf0, pp_trf0]
    pp_tr = nc.alloc_psum_tensor("pp_tr", [128, 512], BF16).ap()
    pp_tr2 = pp_tr
    pp_w = nc.alloc_psum_tensor("pp_w", [128, 1024], F32).ap()

    semnames = ["sPE", "sACT", "sDVE", "sLD", "sST", "sRS", "sGW",
                "sXT0", "sXT1", "sW10", "sW11", "sW20", "sW21",
                "sWA", "sWB", "sWC", "sWD", "sWE", "sWS", "sS1a", "sS1b", "sS2", "sS3", "sRSa", "sRSb"]
    sems, cms = {}, []
    for n in semnames:
        cm = nc.semaphore(n); sems[n] = cm.__enter__(); cms.append(cm)

    cnt = defaultdict(int)
    ops = {e: [] for e in ["sync", "gpsimd", "tensor", "vector", "scalar"]}
    guards = {}

    def emit(eng, fn): ops[eng].append(fn)

    def inc(eng, sem, n, buildfn):
        cnt[sem] += n; v = cnt[sem]
        emit(eng, lambda e, b=buildfn, s=sem, n=n: b(e).then_inc(sems[s], n))
        return (sem, v)

    def wait(eng, mark):
        if mark is None: return
        sem, v = mark
        emit(eng, lambda e, s=sem, v=v: e.wait_ge(sems[s], v))

    def now(sem): return (sem, cnt[sem])

    def gwait(eng, key):
        m = guards.pop(key, None)
        if m is not None: wait(eng, m)

    def gset(key, mark): guards[key] = mark

    for dst, src in [(c_pe, dpehi), (c_ident, dident), (c_zl, dzl), (c_bq, dbq),
                     (c_bk, dbk), (c_bvb, dbvb), (c_bc, dbc),
                     (c_ident_f, didentf), (c_ones1, dones1)]:
        inc("sync", "sLD", 16, lambda e, d=dst, s=src: e.dma_start(out=d, in_=s))
    inc("sync", "sLD", 16, lambda e: e.dma_start(out=xtile[0][:, 0:T] if T <= 1024 else
                                                 xtile[0][:, :], in_=dpelo[:, 0:min(T, 1024)]))
    pe_lo_extra = None
    if T > 1024:
        pe_lo_extra = inc("sync", "sLD", 16, lambda e: e.dma_start(
            out=xtile[1][:, 0:T - 1024], in_=dpelo[:, 1024:T]))
    inc("gpsimd", "sGW", 16, lambda e: e.dma_start(out=c_b1, in_=db1))
    inc("gpsimd", "sGW", 16, lambda e: e.dma_start(out=c_b2, in_=db2))
    inc("gpsimd", "sGW", 16, lambda e: e.dma_start(out=w_c, in_=dwc))
    inc("gpsimd", "sGW", 16, lambda e: e.dma_start(out=w_q, in_=dwq))
    ld_all = now("sLD")
    wait("vector", ld_all)
    m0 = inc("vector", "sDVE", 1, lambda e: e.tensor_tensor(
        c_pe[:, 0:min(T, 1024)], c_pe[:, 0:min(T, 1024)],
        xtile[0][:, 0:min(T, 1024)], ALU.add))
    gset(("xt", 0), m0)
    if T > 1024:
        m1 = inc("vector", "sDVE", 1, lambda e: e.tensor_tensor(
            c_pe[:, 1024:T], c_pe[:, 1024:T], xtile[1][:, 0:T - 1024], ALU.add))
        gset(("xt", 1), m1)
    inc("vector", "sDVE", 1, lambda e: e.memset(c_eps[:, :], 1e-5))
    inc("vector", "sDVE", 1, lambda e: e.memset(c_negC[:, :], -cfg.CSHIFT))
    ones_mark = inc("vector", "sDVE", 1, lambda e: e.memset(
        bass.AP(vS.tensor, vS.offset + 128, [[vS.ap[0][0], 128], [132, NT]]), 1.0))
    consts_v = now("sDVE"); consts_g = now("sGW")
    for eng in ["tensor", "scalar"]:
        wait(eng, consts_v); wait(eng, ld_all); wait(eng, consts_g)
    wait("vector", consts_g)

    XTS = ["sXT0", "sXT1"]; W1S = ["sW10", "sW11"]; W2S = ["sW20", "sW21"]
    bkeys4g = [("ps", 0), ("ps", 1), ("pv", 0), ("pv", 1)]
    banks4g = pp_s + pp_pv
    prev_relu = None
    prev_stores = None
    prev_scatter = None
    st_all_ws = None
    st_wkv = None
    st_s2_all = None; st_s1a = None; st_s1b = None

    def emit_ln_stats_finish():
        wait("vector", now("sACT"))
        wait("vector", now("sDVE"))
        inc("vector", "sDVE", 1, lambda e: e.tensor_tensor(st_v, st_v, st_r, ALU.add))
        inc("vector", "sDVE", 1, lambda e: e.tensor_scalar(st_m, st_m, 1.0 / 1024, None, ALU.mult))
        wait("vector", now("sDVE"))
        inc("vector", "sDVE", 1, lambda e: e.tensor_scalar(st_v, st_v, 1.0 / 1024, None, ALU.mult))
        inc("vector", "sDVE", 1, lambda e: e.tensor_tensor(st_b, st_m, st_m, ALU.mult))
        wait("vector", now("sDVE"))
        lv = inc("vector", "sDVE", 1, lambda e: e.tensor_tensor(st_v, st_v, st_b, ALU.subtract))
        wait("scalar", lv)
        wait("scalar", now("sACT"))
        sv2 = inc("scalar", "sACT", 1, lambda e: e.activation(out=st_v, in_=st_v,
                                                              func=AF.Sqrt, bias=c_eps))
        wait("vector", sv2)
        inc("vector", "sDVE", 1, lambda e: e.reciprocal(st_r, st_v))
        wait("vector", now("sDVE"))
        inc("vector", "sDVE", 1, lambda e: e.tensor_tensor(st_b, st_m, st_r, ALU.mult))
        wait("vector", now("sDVE"))
        bv = inc("vector", "sDVE", 1, lambda e: e.tensor_scalar(st_b, st_b, -1.0, None, ALU.mult))
        return bv

    def emit_ln(p2_act):
        bv = emit_ln_stats_finish()
        wait("scalar", bv)
        wait("scalar", now("sACT"))
        ln1 = None
        for tt in range(NT):
            ln1 = inc("scalar", "sACT", 1, lambda e, tt=tt: e.activation(
                out=x2[:, tt * 1024: tt * 1024 + 1024],
                in_=x2[:, tt * 1024: tt * 1024 + 1024],
                func=AF.Identity, scale=st_r[:, tt: tt + 1], bias=st_b[:, tt: tt + 1]))
        return ln1

    def emit_ln2_finish():
        mred = now("sDVE")
        wait("scalar", mred)
        wait("scalar", now("sACT"))
        gwait("scalar", ("pT", 0)); gwait("scalar", ("pT", 1))
        for tt in range(NT):
            if tt > 0:
                wait("scalar", now("sACT"))
            inc("scalar", "sACT", 1, lambda e, tt=tt: e.activation(
                out=pT[0], in_=x2[:, tt * 1024: tt * 1024 + 512],
                func=AF.Square, accum_out=st_v[:, tt: tt + 1]))
            inc("scalar", "sACT", 1, lambda e, tt=tt: e.activation(
                out=pT[1], in_=x2[:, tt * 1024 + 512: tt * 1024 + 1024],
                func=AF.Square, accum_out=st_r[:, tt: tt + 1]))
        sq2 = now("sACT")
        gset(("pT", 0), sq2); gset(("pT", 1), sq2)
        bv = emit_ln_stats_finish()
        wait("scalar", bv)
        wait("scalar", now("sACT"))
        return bv

    for s in range(NSEQ):
        gwait("gpsimd", "R2xT")
        wait("gpsimd", now("sGW"))
        if prev_relu is not None:
            wait("gpsimd", prev_relu)
        if s == 0:
            wait("gpsimd", consts_v)
        mk_xT = inc("gpsimd", "sGW", 16, lambda e, s=s: e.dma_start(
            out=xT.rearrange("p (c t) -> p c t", c=CH),
            in_=dxT[s].rearrange("(c p) t -> p c t", p=128)))
        if s == 0:
            mk_wkv = inc("gpsimd", "sGW", 16, lambda e: e.dma_start(out=w_kv, in_=dwkv))
            wait("sync", mk_wkv)
            st_wkv = inc("sync", "sS3", 16, lambda e: e.dma_start(out=cwkv, in_=w_kv))
        else:
            wait("sync", st_all_ws)
            wait("sync", st_wkv)
            mk_wkv = inc("sync", "sWE", 16, lambda e: e.dma_start(out=w_kv, in_=cwkv))

        if s == 1:
            wait("vector", st_s2_all); wait("scalar", st_s2_all)
        wait("tensor", mk_wkv)
        if s > 0:
            wait("tensor", mk_xT)
        gwait("tensor", "R3qT")
        gwait("tensor", "R4kv")
        for hc in range(H):
            for tb in range(NQB):
                bk_ = ("ps", tb % 2); ps = pp_s[tb % 2]
                gwait("tensor", bk_)
                mmq = None
                for c in range(CH):
                    mmq = inc("tensor", "sPE", 1, lambda e, ps=ps, c=c, hc=hc, tb=tb: e.matmul(
                        ps[:, :], w_q[:, c * 1024 + hc * 128: c * 1024 + hc * 128 + 128],
                        xT[:, c * T + tb * 512: c * T + tb * 512 + 512],
                        start=(c == 0), stop=(c == CH - 1)))
                wait("vector", mmq)
                ev = inc("vector", "sDVE", 1, lambda e, ps=ps, hc=hc, tb=tb: e.scalar_tensor_tensor(
                    qT[:, hc * T + tb * 512: hc * T + tb * 512 + 512], ps[:, :],
                    c_bq[:, hc: hc + 1], c_pe[:, tb * 512: tb * 512 + 512], ALU.add, ALU.add))
                gset(bk_, ev)
        for tb in range(NQB):
            bk_ = ("ps", tb % 2); ps = pp_s[tb % 2]
            gwait("tensor", bk_)
            mmk = None
            for c in range(CH):
                mmk = inc("tensor", "sPE", 1, lambda e, ps=ps, c=c, tb=tb: e.matmul(
                    ps[:, :], w_kv[:, c * 256: c * 256 + 128],
                    xT[:, c * T + tb * 512: c * T + tb * 512 + 512],
                    start=(c == 0), stop=(c == CH - 1)))
            wait("vector", mmk)
            ev = inc("vector", "sDVE", 1, lambda e, ps=ps, tb=tb: e.scalar_tensor_tensor(
                kT[:, tb * 512: tb * 512 + 512], ps[:, :], c_bk,
                c_pe[:, tb * 512: tb * 512 + 512], ALU.add, ALU.add))
            gset(bk_, ev)
        for tt in range(NT):
            bk_ = ("pv", tt % 2); ps = pp_pv[tt % 2]
            gwait("tensor", bk_)
            mmv = None
            for c in range(CH):
                mmv = inc("tensor", "sPE", 1, lambda e, ps=ps, c=c, tt=tt: e.matmul(
                    ps[:, 0:128], xT[:, c * T + tt * 128: c * T + tt * 128 + 128],
                    w_kv[:, c * 256 + 128: c * 256 + 256],
                    start=(c == 0), stop=(c == CH - 1)))
            wait("vector", mmv)
            if tt == 0:
                if s == 0:
                    wait("vector", ones_mark)
                inc("vector", "sDVE", 1, lambda e: e.memset(
                    bass.AP(vS.tensor, vS.offset + 128,
                            [[vS.ap[0][0], 128], [132, NT]]), 1.0))
            ev = inc("vector", "sDVE", 1, lambda e, ps=ps, tt=tt: e.tensor_tensor(
                vS[:, tt * 132: tt * 132 + 128], ps[:, 0:128], c_bvb, ALU.add))
            gset(bk_, ev)
        p1_done = now("sPE")
        if s == 0:
            wait("gpsimd", p1_done)
            mk_wo = inc("gpsimd", "sGW", 16, lambda e: e.dma_start(out=w_o, in_=dwo))
            mk_tie0 = inc("gpsimd", "sGW", 16, lambda e: e.dma_start(out=c_tie, in_=dtie))
            wait("sync", mk_tie0)
            inc("sync", "sWS", 16, lambda e: e.dma_start(out=cwo, in_=w_o))
            st_tie = inc("sync", "sWS", 16, lambda e: e.dma_start(out=ctie, in_=c_tie))
            mk_tie = mk_tie0
        else:
            wait("sync", p1_done)
            mk_wo = inc("sync", "sWE", 16, lambda e: e.dma_start(out=w_o, in_=cwo))
            mk_tie = inc("sync", "sWE", 16, lambda e: e.dma_start(out=c_tie, in_=ctie))

        ln_apply_marks = {}
        def ln_tile(tt, sq):
            wait("vector", sq)
            wait("vector", now("sDVE"))
            inc("vector", "sDVE", 1, lambda e, tt=tt: e.tensor_tensor(
                st_v[:, tt: tt + 1], st_v[:, tt: tt + 1], st_r[:, tt: tt + 1], ALU.add))
            inc("vector", "sDVE", 1, lambda e, tt=tt: e.tensor_scalar(
                st_m[:, tt: tt + 1], st_m[:, tt: tt + 1], 1.0 / 1024, None, ALU.mult))
            wait("vector", now("sDVE"))
            inc("vector", "sDVE", 1, lambda e, tt=tt: e.tensor_scalar(
                st_v[:, tt: tt + 1], st_v[:, tt: tt + 1], 1.0 / 1024, None, ALU.mult))
            inc("vector", "sDVE", 1, lambda e, tt=tt: e.tensor_tensor(
                st_b[:, tt: tt + 1], st_m[:, tt: tt + 1], st_m[:, tt: tt + 1], ALU.mult))
            wait("vector", now("sDVE"))
            lv = inc("vector", "sDVE", 1, lambda e, tt=tt: e.tensor_tensor(
                st_v[:, tt: tt + 1], st_v[:, tt: tt + 1], st_b[:, tt: tt + 1],
                ALU.subtract))
            wait("scalar", lv)
            sv2 = inc("scalar", "sACT", 1, lambda e, tt=tt: e.activation(
                out=st_v[:, tt: tt + 1], in_=st_v[:, tt: tt + 1],
                func=AF.Sqrt, bias=c_eps))
            wait("vector", sv2)
            inc("vector", "sDVE", 1, lambda e, tt=tt: e.reciprocal(
                st_r[:, tt: tt + 1], st_v[:, tt: tt + 1]))
            wait("vector", now("sDVE"))
            inc("vector", "sDVE", 1, lambda e, tt=tt: e.tensor_tensor(
                st_b[:, tt: tt + 1], st_m[:, tt: tt + 1], st_r[:, tt: tt + 1], ALU.mult))
            wait("vector", now("sDVE"))
            bv = inc("vector", "sDVE", 1, lambda e, tt=tt: e.tensor_scalar(
                st_b[:, tt: tt + 1], st_b[:, tt: tt + 1], -1.0, None, ALU.mult))
            wait("vector", now("sDVE"))
            am = inc("vector", "sDVE", 1, lambda e, tt=tt: e.tensor_scalar(
                x2[:, tt * 1024: tt * 1024 + 1024],
                x2[:, tt * 1024: tt * 1024 + 1024],
                st_r[:, tt: tt + 1], st_b[:, tt: tt + 1], ALU.mult, ALU.add))
            ln_apply_marks[tt] = am

        trb = [pp_trf[0], pp_trf[1]]
        trbB = [pp_tr, pp_tr2]
        def emit_logits(tb, s):
            if tb == 0:
                wait("tensor", mk_tie)
                if s == 0:
                    wait("vector", st_tie)
            gwait("tensor", ("x2t",))
            gwait("scalar", ("ctile",))
            if s > 0 and tb == 0:
                wait("scalar", prev_scatter)
            cmk = None
            for t4 in range(4):
                tt = tb * 4 + t4
                wait("scalar", ln_apply_marks[tt])
                cmk = inc("scalar", "sACT", 1, lambda e, tt=tt, t4=t4: e.activation(
                    out=casttile[:, t4 * 1024: t4 * 1024 + 1024],
                    in_=x2[:, tt * 1024: tt * 1024 + 1024], func=AF.Copy))
            wait("tensor", cmk)
            cvs = {}
            for c in range(CH):
                gwait("tensor", ("tr", 0))
                tvv = None
                for t4 in range(4):
                    tvv = inc("tensor", "sPE", 1, lambda e, c=c, t4=t4: e.transpose(
                        pp_tr[:, t4 * 128: t4 * 128 + 128],
                        casttile[:, t4 * 1024 + c * 128: t4 * 1024 + c * 128 + 128],
                        c_ident))
                wait("vector", tvv)
                cv = inc("vector", "sDVE", 1, lambda e, c=c: e.tensor_copy(
                    x2t_stage[:, c * 512: c * 512 + 512], pp_tr[:, :]))
                gset(("tr", 0), cv)
                cvs[c] = cv
            gset(("ctile",), cvs[CH - 1])
            gwait("tensor", ("pw", 0))
            lgv = None
            for c in range(CH):
                wait("tensor", cvs[c])
                lgv = inc("tensor", "sPE", 1, lambda e, c=c: e.matmul(
                    pp_w[:, 0:512], w_c[:, c * 128: c * 128 + 128],
                    x2t_stage[:, c * 512: c * 512 + 512], start=(c == 0), stop=False))
            lgv = inc("tensor", "sPE", 1, lambda e, tb=tb: e.matmul(
                pp_w[:, 0:512], c_ones1, c_tie[:, tb * 512: tb * 512 + 512],
                start=False, stop=True))
            gset(("x2t",), lgv)
            wait("vector", lgv)
            lev = inc("vector", "sDVE", 1, lambda e, tb=tb: e.tensor_scalar(
                lg[:, tb * 512: tb * 512 + 512], pp_w[:, 0:512], c_bc, None, ALU.add))
            gset(("pw", 0), lev)

        for qb in range(NQB):
            epi = {"pv_last": None}
            def emit_scores(h, kc, exp_m):
                ping = kc % 2
                gwait("tensor", ("ps", ping))
                if kc >= 2:
                    wait("tensor", exp_m[kc - 2])
                return inc("tensor", "sPE", 1, lambda e, kc=kc, ping=ping, h=h, qb=qb: e.matmul(
                    pp_s[ping][:, :], kT[:, kc * 128: kc * 128 + 128],
                    qT[:, h * T + qb * 512: h * T + qb * 512 + 512],
                    start=True, stop=True))
            def emit_exp_pair(h, kc, sv, exp_m, mask_m):
                wait("scalar", sv)
                gwait("scalar", ("pT", 0)); gwait("scalar", ("pT", 1))
                ev = inc("scalar", "sACT", 1, lambda e: e.activation(
                    out=pTp[:, :], in_=pp_sp[:, :], func=AF.Exp,
                    bias=c_negC, scale=cfg.SCALE))
                exp_m[kc] = ev; exp_m[kc + 1] = ev
                gset(("ps", 0), ev); gset(("ps", 1), ev)
                for k2 in (kc, kc + 1):
                    j = k2 - 4 * qb
                    if j >= 0:
                        wait("vector", ev)
                        mv = inc("vector", "sDVE", 1, lambda e, ping=k2 % 2, j=j: e.tensor_tensor(
                            pT[ping], pT[ping],
                            c_zl[:, (3 - j) * 128: (3 - j) * 128 + 512], ALU.mult))
                        mask_m[k2] = mv
            def emit_pv(h, kc, kcmax, exp_m, mask_m):
                ping = kc % 2
                if kc in mask_m:
                    wait("tensor", mask_m[kc])
                else:
                    wait("tensor", exp_m[kc])
                if kc == 0:
                    gwait("tensor", ("pv", 0)); gwait("tensor", ("pv", 1))
                pvl = None
                for sub in range(4):
                    bank = pp_pv[sub // 2]; colo = (sub % 2) * 129
                    pvl = inc("tensor", "sPE", 1, lambda e, bank=bank, colo=colo,
                              ping=ping, sub=sub, kc=kc, kcmax=kcmax: e.matmul(
                        bank[:, colo: colo + 129],
                        pT[ping][:, sub * 128: sub * 128 + 128],
                        vS[:, kc * 132: kc * 132 + 129],
                        start=(kc == 0 and sub % 2 == 0), stop=(kc == kcmax - 1),
                        skip_group_check=True))
                gset(("pT", ping), pvl)
                return pvl
            def emit_epilogue_pre(h):
                wait("vector", epi["pv_last"])
                gwait("vector", ("rs",))
                rv = None
                for sub in range(4):
                    bank = pp_pv[sub // 2]; colo = (sub % 2) * 129
                    rv = inc("vector", "sDVE", 1, lambda e, bank=bank, colo=colo, sub=sub:
                             e.reciprocal(rs[:, sub: sub + 1], bank[:, colo + 128: colo + 129]))
                wait("vector", rv)
                gwait("vector", ("pTn",))
                nm = None
                for sub in range(4):
                    bank = pp_pv[sub // 2]; colo = (sub % 2) * 129
                    nm = inc("vector", "sDVE", 1, lambda e, bank=bank, colo=colo, sub=sub:
                             e.tensor_scalar(pTn[:, sub * 128: sub * 128 + 128],
                                             bank[:, colo: colo + 128],
                                             rs[:, sub: sub + 1], None, ALU.mult))
                gset(("pv", 0), nm); gset(("pv", 1), nm)
                gset(("rs",), nm)
                epi["nm"] = nm
            def emit_epilogue_tensor(h):
                wait("tensor", epi["nm"])
                gwait("tensor", ("tr", 0))
                tv = None
                for sub in range(4):
                    tv = inc("tensor", "sPE", 1, lambda e, sub=sub: e.transpose(
                        pp_tr[:, sub * 128: sub * 128 + 128],
                        pTn[:, sub * 128: sub * 128 + 128], c_ident))
                gset(("pTn",), tv)
                wait("vector", tv)
                av = inc("vector", "sDVE", 1, lambda e, h=h: e.tensor_copy(
                    atT[:, h * 512: h * 512 + 512], pp_tr[:, :]))
                gset(("tr", 0), av)
                epi["av"] = av
            for h in range(H):
                kcmax = (qb + 1) * 4
                exp_m = {}; mask_m = {}
                emit_scores(h, 0, exp_m)
                sv = emit_scores(h, 1, exp_m)
                emit_exp_pair(h, 0, sv, exp_m, mask_m)
                if h > 0:
                    emit_epilogue_tensor(h - 1)
                for kc0 in range(0, kcmax, 2):
                    sv = None
                    if kc0 + 2 < kcmax:
                        emit_scores(h, kc0 + 2, exp_m)
                        sv = emit_scores(h, kc0 + 3, exp_m)
                    epi["pv_last"] = emit_pv(h, kc0, kcmax, exp_m, mask_m)
                    epi["pv_last"] = emit_pv(h, kc0 + 1, kcmax, exp_m, mask_m)
                    if sv is not None:
                        emit_exp_pair(h, kc0 + 2, sv, exp_m, mask_m)
                emit_epilogue_pre(h)
            emit_epilogue_tensor(H - 1)
            av = epi["av"]
            wait("tensor", mk_tie)
            wait("tensor", av)
            for t4 in range(4):
                tt = qb * 4 + t4
                xs = XTS[tt % 2]
                if qb == 0 and t4 == 0:
                    wait("sync", p1_done)
                    if s == 0:
                        wait("sync", st_wkv)
                gwait("sync", ("xt", tt % 2))
                xv = inc("sync", xs, 16, lambda e, s=s, tt=tt: e.dma_start(
                    out=xtile[tt % 2], in_=dx[s, tt * 128: tt * 128 + 128, :]))
                gwait("tensor", ("pw", 0)); gwait("tensor", ("pw", 1))
                wv = None
                for h in range(H):
                    for half in range(2):
                        wv = inc("tensor", "sPE", 1, lambda e, h=h, half=half, t4=t4: e.matmul(
                            pp_w[:, half * 512: half * 512 + 512],
                            atT[:, h * 512 + t4 * 128: h * 512 + t4 * 128 + 128],
                            w_o[:, h * 1024 + half * 512: h * 1024 + half * 512 + 512],
                            start=(h == 0), stop=(h == H - 1), skip_group_check=True))
                wait("vector", wv); wait("vector", xv)
                if qb == 0 and t4 == 0 and s > 0:
                    wait("vector", prev_stores)
                rv2 = inc("vector", "sDVE", 1, lambda e, tt=tt: e.tensor_tensor(
                    x2[:, tt * 1024: tt * 1024 + 1024], pp_w[:, :],
                    xtile[tt % 2][:, :], ALU.add))
                gset(("pw", 0), rv2); gset(("pw", 1), rv2)
                gset(("xt", tt % 2), rv2)
                wait("vector", rv2)
                inc("vector", "sDVE", 1, lambda e, tt=tt: e.tensor_reduce(
                    st_m[:, tt: tt + 1], x2[:, tt * 1024: tt * 1024 + 1024], AX.X, ALU.add))
                wait("scalar", rv2)
                gwait("scalar", ("pT", 0)); gwait("scalar", ("pT", 1))
                sq = inc("scalar", "sACT", 1, lambda e, tt=tt: e.activation(
                    out=pT[0], in_=x2[:, tt * 1024: tt * 1024 + 512],
                    func=AF.Square, accum_out=st_v[:, tt: tt + 1]))
                sq = inc("scalar", "sACT", 1, lambda e, tt=tt: e.activation(
                    out=pT[1], in_=x2[:, tt * 1024 + 512: tt * 1024 + 1024],
                    func=AF.Square, accum_out=st_r[:, tt: tt + 1]))
                gset(("pT", 0), sq); gset(("pT", 1), sq)
                ln_tile(tt, sq)
            emit_logits(qb, s)
        ln1 = ln_apply_marks[NT - 1]

        wait("gpsimd", ln1)
        w1sems = [["sW10", "sW11"], ["sWA", "sWB"]]
        w1_store = {}
        def w1_load(q4):
            if s == 0:
                if q4 >= 2:
                    wait("gpsimd", w1_store[q4 - 2])
                m = inc("gpsimd", w1sems[0][q4 % 2], 16,
                        lambda e, q=q4: e.dma_start(out=w_1[q % 2], in_=dw1[q]))
                wait("sync", m)
                w1_store[q4] = inc("sync", ["sS1a", "sS1b"][q4 % 2], 16,
                                   lambda e, q=q4: e.dma_start(out=cw1[q], in_=w_1[q % 2]))
                return m
            return inc("scalar", w1sems[1][q4 % 2], 16,
                       lambda e, q=q4: e.dma_start(out=w_1[q % 2], in_=cw1[q]))
        if s == 1:
            wait("scalar", st_all_ws)
            wait("scalar", st_s2_all)
            wait("scalar", st_s1a); wait("scalar", st_s1b)
        gw1 = {0: w1_load(0)}
        def w2t_load(q4, part):
            c0 = 0 if part == 0 else 2048
            if s == 0:
                sem = "sW20" if part == 0 else "sW21"
                inc("gpsimd", sem, 16, lambda e, q4=q4, c0=c0: e.dma_start(
                    out=w_2t[0][:, c0: c0 + 2048], in_=dw2[q4][:, c0: c0 + 2048]))
                m = inc("gpsimd", sem, 16, lambda e, q4=q4, c0=c0: e.dma_start(
                    out=w_2t[0][:, 4096 + c0: 4096 + c0 + 2048],
                    in_=dw2[q4][:, 4096 + c0: 4096 + c0 + 2048]))
                wait("sync", m)
                inc("sync", "sS2", 16, lambda e, q4=q4, c0=c0: e.dma_start(
                    out=cw2[q4][:, c0: c0 + 2048], in_=w_2t[0][:, c0: c0 + 2048]))
                w2t_store[q4] = inc("sync", "sS2", 16, lambda e, q4=q4, c0=c0: e.dma_start(
                    out=cw2[q4][:, 4096 + c0: 4096 + c0 + 2048],
                    in_=w_2t[0][:, 4096 + c0: 4096 + c0 + 2048]))
                return m
            sem = "sWC" if part == 0 else "sWD"
            inc("scalar", sem, 16, lambda e, q4=q4, c0=c0: e.dma_start(
                out=w_2t[0][:, c0: c0 + 2048], in_=cw2[q4][:, c0: c0 + 2048]))
            return inc("scalar", sem, 16, lambda e, q4=q4, c0=c0: e.dma_start(
                out=w_2t[0][:, 4096 + c0: 4096 + c0 + 2048],
                in_=cw2[q4][:, 4096 + c0: 4096 + c0 + 2048]))
        w2t_store = {}
        gw2 = {0: w2t_load(0, 0)}
        gw2b = {0: w2t_load(0, 1)}
        wait("vector", now("sDVE"))
        for g in range(NG):
            inc("vector", "sDVE", 1, lambda e, g=g: e.tensor_reduce(
                gmax[:, g: g + 1], lg[:, g * 32: g * 32 + 32], AX.X, ALU.max))
        gv = now("sDVE"); wait("vector", gv)
        pv2 = None
        for g in range(NG):
            pv2 = inc("vector", "sDVE", 1, lambda e, g=g: e.tensor_single_scalar(
                pmT[:, g * 32: g * 32 + 32], lg[:, g * 32: g * 32 + 32],
                gmax[:, g: g + 1], ALU.is_equal))
        wait("tensor", pv2)
        pmv = None
        for tt in range(NT):
            bnk = trbB[tt % 2]
            gwait("tensor", ("tr", 0))
            tv2 = inc("tensor", "sPE", 1, lambda e, tt=tt, bnk=bnk: e.transpose(
                bnk[:, 0:128], pmT[:, tt * 128: tt * 128 + 128], c_ident))
            wait("vector", tv2)
            pmv = inc("vector", "sDVE", 1, lambda e, tt=tt, bnk=bnk: e.tensor_copy(
                pmt[:, tt * 128: tt * 128 + 128], bnk[:, 0:128]))
            gset(("tr", 0), pmv)
        e1p_marks = {}
        for q4 in range(4):
            wait("tensor", gw1[q4])
            if q4 + 1 < 4:
                if q4 >= 1:
                    wait("gpsimd" if s == 0 else "scalar", now("sPE"))
                gw1[q4 + 1] = w1_load(q4 + 1)
            if q4 == 0:
                wait("tensor", pmv)
            gev = {}
            for tt in range(NT):
                if q4 == 0 and tt == 0:
                    wait("vector", pv2)
                gwait("vector", ("x2b", tt % 2))
                cm2 = inc("vector", "sDVE", 1, lambda e, tt=tt, q4=q4: e.tensor_copy(
                    x2b[tt % 2], x2[:, tt * 1024 + q4 * 256: tt * 1024 + q4 * 256 + 256]))
                wait("tensor", cm2)
                lastg = None
                for j in range(4):
                    g = tt * 4 + j
                    bidx = g % 4
                    gwait("tensor", bkeys4g[bidx])
                    gvv = None
                    for cc in range(2):
                        gvv = inc("tensor", "sPE", 1, lambda e, j=j, cc=cc, tt=tt, bidx=bidx: e.matmul(
                            banks4g[bidx][:, cc * 128: cc * 128 + 128],
                            x2b[tt % 2][j * 32: j * 32 + 32, cc * 128: cc * 128 + 128],
                            pmt[j * 32: j * 32 + 32, tt * 128: tt * 128 + 128],
                            start=(cc == 0), stop=True,
                            tile_position=(32 * j, 0), skip_group_check=True))
                    lastg = gvv
                    eng2 = "vector" if (g % 2 == 0) else "scalar"
                    wait(eng2, gvv)
                    sem2 = "sDVE" if eng2 == "vector" else "sACT"
                    cp2 = (lambda e: e.tensor_copy) if eng2 == "vector" else (lambda e: e.copy)
                    m = inc(eng2, sem2, 1, lambda e, g=g, bidx=bidx, cp2=cp2: cp2(e)(
                        bass.AP(gq.tensor, gq.offset + g * 128,
                                [[gq.ap[0][0], 128], [NG * 128, 2], [1, 128]]),
                        banks4g[bidx][:, 0:256].rearrange("p (a b) -> p a b", b=128)))
                    gev[g] = m
                    gset(bkeys4g[bidx], m)
                gset(("x2b", tt % 2), lastg)
            for r in range(4):
                bk_ = ("ps", r % 2); ps = pp_s[r % 2]
                gwait("tensor", bk_)
                e1v = None
                for qq in range(8):
                    qidx = r * 8 + qq
                    for b in range(4):
                        es = qidx * 4 + b
                        if qq == 0 and b == 0:
                            wait("tensor", gev[NG - 1])
                            wait("tensor", gev[NG - 2])
                        for cc in range(2):
                            e1v = inc("tensor", "sPE", 1, lambda e, ps=ps, b=b, qq=qq, es=es,
                                      cc=cc, q4=q4: e.matmul(
                                ps[b * 32: b * 32 + 32, qq * 64: qq * 64 + 64],
                                w_1[q4 % 2][:, cc * 4096 + es * 32: cc * 4096 + es * 32 + 32],
                                bass.AP(gq.tensor, gq.offset + cc * NG * 128 + es,
                                        [[gq.ap[0][0], 128], [128, 64]]),
                                start=(cc == 0), stop=(cc == 1),
                                tile_position=(0, 32 * b), skip_group_check=True))
                wait("vector", e1v)
                if q4 > 0:
                    wait("vector", e1p_marks[r])
                if q4 == 0:
                    if s == 0 and r == 0:
                        wait("vector", st_wkv)
                    m = inc("vector", "sDVE", 1, lambda e, r=r, ps=ps: e.scalar_tensor_tensor(
                        e1p[:, r * 512: r * 512 + 512], ps[:, :], 0.0,
                        c_b1[:, r * 512: r * 512 + 512], ALU.add, ALU.add))
                else:
                    m = inc("vector", "sDVE", 1, lambda e, r=r, ps=ps: e.tensor_tensor(
                        e1p[:, r * 512: r * 512 + 512], e1p[:, r * 512: r * 512 + 512],
                        ps[:, :], ALU.add))
                e1p_marks[r] = m
                gset(bk_, m)
                if q4 == 3:
                    wait("scalar", m)
                    inc("scalar", "sACT", 1, lambda e, r=r: e.activation(
                        out=inr[:, r * 512: r * 512 + 512],
                        in_=e1p[:, r * 512: r * 512 + 512], func=AF.Relu))
        relu_done = now("sACT")

        banks4 = [pp_s[0], pp_s[1], pp_pv[0], pp_pv[1]]
        bkeys4 = [("ps", 0), ("ps", 1), ("pv", 0), ("pv", 1)]
        sc_last = None
        rs_marks = {}
        rsq_all = {}

        def emit_e2(q4):
            wait("tensor", gw2[q4])
            if q4 == 0:
                wait("tensor", relu_done)
                if s == 0:
                    for eng_ in ("vector", "scalar"):
                        wait(eng_, w1_store[2]); wait(eng_, w1_store[3])
            rsq_marks = []
            rpn_guard = guards.pop(("rpn",), None)
            for r in range(64):
                bidx = r % 4; ps = banks4[bidx]
                gwait("tensor", bkeys4[bidx])
                if r == 32:
                    wait("tensor", gw2b[q4])
                e2v = None
                for half in range(2):
                    es = r + 64 * half
                    b = es % 4; q = es // 4
                    e2v = inc("tensor", "sPE", 1, lambda e, ps=ps, b=b, q=q, half=half,
                              q4=q4: e.matmul(
                        ps[half * 64: half * 64 + 64, 0:256],
                        inr[b * 32: b * 32 + 32, q * 64: q * 64 + 64],
                        w_2t[0][b * 32: b * 32 + 32, q * 256: q * 256 + 256],
                        start=True, stop=True,
                        tile_position=(b * 32, 64 * half),
                        skip_group_check=True))
                eng2 = "vector" if (r % 2 == 0) else "scalar"
                sem2 = "sDVE" if eng2 == "vector" else "sACT"
                wait(eng2, e2v)
                if rpn_guard is not None:
                    wait(eng2, rpn_guard)
                cp3 = (lambda e: e.tensor_copy) if eng2 == "vector" else (lambda e: e.copy)
                m = inc(eng2, sem2, 1, lambda e, r=r, ps=ps, cp3=cp3: cp3(e)(
                    rpn[:, r * 256: r * 256 + 256], ps[:, 0:256]))
                gset(bkeys4[bidx], m)
                rsq_marks.append(m)
            if q4 + 1 < 4:
                eng_w = "gpsimd" if s == 0 else "scalar"
                wait(eng_w, now("sPE"))
                if s == 0:
                    wait("gpsimd", w2t_store[q4])
                gw2[q4 + 1] = w2t_load(q4 + 1, 0)
                gw2b[q4 + 1] = w2t_load(q4 + 1, 1)
            rsq_all[q4] = rsq_marks

        def emit_restack(q4):
            gwait("scalar", ("rpe",))
            for mm_ in rsq_all[q4]:
                wait("sync", mm_)
            for hh in range(2):
                sth = inc("sync", ["sRSa", "sRSb"][hh], 16, lambda e, hh=hh: e.dma_start(
                    out=bass.AP(drp.tensor, drp.offset + 64 * hh * 16384,
                                [[16384, 64], [1, 16384]]),
                    in_=rpn[64 * hh: 64 * hh + 64, :]))
                wait("scalar", sth)
                inc("scalar", "sRS", 16, lambda e, hh=hh: e.dma_start(
                    out=rpe[64 * hh: 64 * hh + 64, :].rearrange("p (g d) -> p g d", d=256),
                    in_=bass.AP(drp.tensor, drp.offset + 64 * hh * 16384,
                                [[256, 64], [16384, NG], [1, 256]])))
            rs_marks[q4] = now("sRS")
            gset(("rpn",), rs_marks[q4])

        def emit_scatter(q4):
            nonlocal sc_last
            wait("tensor", rs_marks[q4])
            for tt in range(NT):
                colo = 512 * (tt % 2)
                gwait("tensor", ("pw", tt % 2))
                inc("tensor", "sPE", 1, lambda e, tt=tt, q4=q4, colo=colo: e.matmul(
                    pp_w[:, colo: colo + 256], pmT[:, tt * 128: tt * 128 + 128],
                    c_b2[:, q4 * 256: q4 * 256 + 256], start=True, stop=False,
                    skip_group_check=True))
                scv = None
                for j in range(4):
                    g = tt * 4 + j
                    scv = inc("tensor", "sPE", 1, lambda e, tt=tt, j=j, g=g, colo=colo: e.matmul(
                        pp_w[j * 32: j * 32 + 32, colo: colo + 256],
                        pmT[:, tt * 128 + j * 32: tt * 128 + j * 32 + 32],
                        rpe[:, g * 256: g * 256 + 256],
                        start=False, stop=True,
                        tile_position=(0, 32 * j), skip_group_check=True))
                wait("vector", scv)
                m = inc("vector", "sDVE", 1, lambda e, tt=tt, q4=q4, colo=colo: e.tensor_tensor(
                    x2[:, tt * 1024 + q4 * 256: tt * 1024 + q4 * 256 + 256],
                    x2[:, tt * 1024 + q4 * 256: tt * 1024 + q4 * 256 + 256],
                    pp_w[:, colo: colo + 256], ALU.add))
                gset(("pw", tt % 2), m)
                sc_last = m
            gset(("rpe",), now("sPE"))

        emit_e2(0)
        emit_restack(0)
        emit_e2(1)
        emit_scatter(0)
        emit_restack(1)
        emit_e2(2)
        emit_scatter(1)
        emit_restack(2)
        emit_e2(3)
        emit_scatter(2)
        emit_restack(3)
        emit_scatter(3)
        wait("vector", sc_last)
        wait("vector", now("sDVE"))
        for tt in range(NT):
            inc("vector", "sDVE", 1, lambda e, tt=tt: e.tensor_reduce(
                st_m[:, tt: tt + 1], x2[:, tt * 1024: tt * 1024 + 1024], AX.X, ALU.add))
        if s == 0:
            wait("scalar", w2t_store[3])
        _ = emit_ln2_finish()
        st_marks = []
        for tt in range(NT):
            am = inc("scalar", "sACT", 1, lambda e, tt=tt: e.activation(
                out=x2[:, tt * 1024: tt * 1024 + 1024],
                in_=x2[:, tt * 1024: tt * 1024 + 1024],
                func=AF.Identity, scale=st_r[:, tt: tt + 1], bias=st_b[:, tt: tt + 1]))
            wait("sync", am)
            stm = inc("sync", "sST", 16, lambda e, s=s, tt=tt: e.dma_start(
                out=dy[s, tt * 128: tt * 128 + 128, :], in_=x2[:, tt * 1024: tt * 1024 + 1024]))
            st_marks.append(stm)
        prev_stores = st_marks[-1]
        prev_scatter = now("sPE")
        if s == 0:
            st_all_ws = now("sWS")
            st_s2_all = now("sS2")
            st_s1a = now("sS1a"); st_s1b = now("sS1b")
        gset("R2xT", sc_last)
        gset("R4kv", now("sPE"))
        prev_relu = relu_done

    final_st = now("sST")
    wait("gpsimd", final_st)
    wait("gpsimd", now("sRS"))
    wait("sync", final_st)

    with nc.Block() as block:
        @block.sync
        def _(e):
            for fn in ops["sync"]: fn(e)
        @block.gpsimd
        def _(e):
            for fn in ops["gpsimd"]: fn(e)
        @block.tensor
        def _(e):
            for fn in ops["tensor"]: fn(e)
        @block.vector
        def _(e):
            for fn in ops["vector"]: fn(e)
        @block.scalar
        def _(e):
            for fn in ops["scalar"]: fn(e)
    for cm in cms:
        cm.__exit__(None, None, None)
    return nc


def host_constants(cfg):
    T = cfg.T
    pos = np.arange(T, dtype=np.float64)[:, None]
    inv_freq = np.exp(-math.log(10000.0) * np.arange(0, 128, 2, dtype=np.float64) / 128)
    ang = pos * inv_freq[None, :]
    pe = np.zeros((T, 128), dtype=np.float64)
    pe[:, 0::2] = np.sin(ang)
    pe[:, 1::2] = np.cos(ang)
    peT = pe.T.copy()
    pehi = peT.astype(np.float32)
    pelo = (peT - pehi.astype(np.float64)).astype(np.float32)
    tie = np.tile(np.linspace(0.0, 1e-6, 32, dtype=np.float64), T // 32).astype(BF)[None]
    ident = np.eye(128, dtype=BF)
    identf = np.eye(128, dtype=np.float32)
    k = np.arange(128)[:, None]
    c = np.arange(896)[None, :]
    zl = (c >= k + 384).astype(BF)
    ones1 = np.ones((1, 128), dtype=BF)
    return dict(pehi=pehi, pelo=pelo, tie=tie, ident=ident,
                zl=zl, ones1=ones1, identf=identf)


def host_weights(cfg, W):
    D = cfg.D
    wq = np.ascontiguousarray(W["Wq"].reshape(8, 128, 1024).transpose(1, 0, 2)
                              ).reshape(128, 8 * 1024).astype(np.float32)
    wkv = np.ascontiguousarray(W["Wkv"].reshape(8, 128, 256).transpose(1, 0, 2)
                               ).reshape(128, 8 * 256).astype(np.float32)
    wo = np.ascontiguousarray(W["Wo"].reshape(8, 128, 1024).transpose(1, 0, 2)
                              ).reshape(128, 8 * 1024).astype(np.float32)
    wc = np.ascontiguousarray(W["Wc"].reshape(1024, 128).reshape(8, 128, 128)
                              .transpose(1, 0, 2)).reshape(128, 8 * 128).astype(np.float32)
    W1 = W["W1"].reshape(128, 1024, 32)
    w1 = W1.reshape(128, 4, 2, 128, 32).transpose(1, 3, 2, 0, 4)
    w1 = np.ascontiguousarray(w1).reshape(4, 128, 2 * 128 * 32).astype(np.float32)
    W2 = W["W2"].reshape(128, 1024, 32)
    w2 = np.zeros((4, 128, 32 * 256), dtype=np.float32)
    es = np.arange(128)
    b = es % 4; q = es // 4
    for q4 in range(4):
        blk = W2[:, q4 * 256:(q4 + 1) * 256, :]
        for e_ in range(128):
            w2[q4, b[e_] * 32:(b[e_] + 1) * 32,
               (q[e_] // 2) * 512 + (q[e_] % 2) * 256:
               (q[e_] // 2) * 512 + (q[e_] % 2) * 256 + 256] = blk[e_].T
    bq = np.ascontiguousarray(W["bq"].reshape(8, 128).T).astype(np.float32)
    bk = W["bkv"][:128].reshape(128, 1).astype(np.float32)
    bvb = np.broadcast_to(W["bkv"][128:], (128, 128)).copy().astype(np.float32)
    bc = W["bc"].reshape(128, 1).astype(np.float32)
    b1f = W["b1"].reshape(128, 32).astype(np.float32)
    b1 = np.zeros((128, 2048), dtype=np.float32)
    for es in range(128):
        b = es % 4; q = es // 4
        b1[b * 32:(b + 1) * 32, q * 64:(q + 1) * 64] = b1f[es][:, None]
    b2 = W["b2"].reshape(128, 1024).astype(np.float32)
    return dict(wq=wq, wkv=wkv, wo=wo, wc=wc, w1=w1, w2t=w2,
                bq=bq, bk=bk, bvb=bvb, bc=bc, b1=b1, b2=b2)


from concourse.bass_utils import run_bass_kernel_spmd

_CFG = Cfg(T=2048, NSEQ=4)
_NC_CACHE = [None]


def kernel(L0_attn_x, L0_attn_Wq, L0_attn_bq, L0_attn_Wkv, L0_attn_bkv, L0_attn_Wo,
           L0_ffn_Wc, L0_ffn_bc, L0_ffn_W1, L0_ffn_b1, L0_ffn_W2, L0_ffn_b2):
    cfg = _CFG
    x = np.ascontiguousarray(np.asarray(L0_attn_x, dtype=np.float32))
    B = x.shape[0]
    n_cores = 8
    per = B // n_cores
    W = dict(Wq=np.asarray(L0_attn_Wq), bq=np.asarray(L0_attn_bq),
             Wkv=np.asarray(L0_attn_Wkv), bkv=np.asarray(L0_attn_bkv),
             Wo=np.asarray(L0_attn_Wo), Wc=np.asarray(L0_ffn_Wc),
             bc=np.asarray(L0_ffn_bc), W1=np.asarray(L0_ffn_W1),
             b1=np.asarray(L0_ffn_b1), W2=np.asarray(L0_ffn_W2),
             b2=np.asarray(L0_ffn_b2))
    wl = host_weights(cfg, W)
    consts = host_constants(cfg)
    if _NC_CACHE[0] is None:
        _NC_CACHE[0] = build_nc(cfg)
    nc = _NC_CACHE[0]
    in_maps = []
    for c in range(n_cores):
        xs = x[c * per:(c + 1) * per]
        in_maps.append(dict(x=xs, xT=np.ascontiguousarray(xs.transpose(0, 2, 1)),
                            **wl, **consts))
    res = run_bass_kernel_spmd(nc, in_maps, core_ids=list(range(n_cores)))
    return np.concatenate([np.asarray(r["y"]) for r in res.results], axis=0)


# version 13 (final)
# speedup vs baseline: 1.1356x; 1.1356x over previous
import math
from collections import defaultdict
import numpy as np
import ml_dtypes
import concourse.bass as bass
import concourse.mybir as mybir

F32 = mybir.dt.float32
BF16 = mybir.dt.bfloat16
AF = mybir.ActivationFunctionType
ALU = mybir.AluOpType
AX = mybir.AxisListType
BF = ml_dtypes.bfloat16
KB = 1024


class Cfg:
    def __init__(self, T=2048, NSEQ=4):
        assert T % 512 == 0
        self.T = T; self.NSEQ = NSEQ
        self.D = 1024; self.H = 8; self.HD = 128; self.CH = 8
        self.NT = T // 128; self.NQB = T // 512; self.NG = T // 32
        self.SCALE = 1.0 / math.sqrt(128.0)
        self.CSHIFT = 12.0


def build_nc(cfg):
    T, NSEQ, H, CH, NT, NQB, NG = (cfg.T, cfg.NSEQ, cfg.H, cfg.CH, cfg.NT,
                                   cfg.NQB, cfg.NG)
    nc = bass.Bass("TRN2", target_bir_lowering=False)

    dx = nc.dram_tensor("x", [NSEQ, T, cfg.D], F32, kind="ExternalInput").ap()
    dxT = nc.dram_tensor("xT", [NSEQ, cfg.D, T], F32, kind="ExternalInput").ap()
    dwq = nc.dram_tensor("wq", [128, CH * 1024], F32, kind="ExternalInput").ap()
    dwkv = nc.dram_tensor("wkv", [128, CH * 256], F32, kind="ExternalInput").ap()
    dwo = nc.dram_tensor("wo", [128, CH * 1024], F32, kind="ExternalInput").ap()
    dwc = nc.dram_tensor("wc", [128, CH * 128], F32, kind="ExternalInput").ap()
    dw1 = nc.dram_tensor("w1", [4, 128, 2 * 128 * 32], F32, kind="ExternalInput").ap()
    dw2 = nc.dram_tensor("w2t", [4, 128, 32 * 256], F32, kind="ExternalInput").ap()
    dbq = nc.dram_tensor("bq", [128, CH], F32, kind="ExternalInput").ap()
    dbk = nc.dram_tensor("bk", [128, 1], F32, kind="ExternalInput").ap()
    dbvb = nc.dram_tensor("bvb", [128, 128], F32, kind="ExternalInput").ap()
    dbc = nc.dram_tensor("bc", [128, 1], F32, kind="ExternalInput").ap()
    db1 = nc.dram_tensor("b1", [128, 2048], F32, kind="ExternalInput").ap()
    db2 = nc.dram_tensor("b2", [128, 1024], F32, kind="ExternalInput").ap()
    dpehi = nc.dram_tensor("pehi", [128, T], F32, kind="ExternalInput").ap()
    dpelo = nc.dram_tensor("pelo", [128, T], F32, kind="ExternalInput").ap()
    dtie = nc.dram_tensor("tie", [1, T], BF16, kind="ExternalInput").ap()
    dones1 = nc.dram_tensor("ones1", [1, 128], BF16, kind="ExternalInput").ap()
    dident = nc.dram_tensor("ident", [128, 128], BF16, kind="ExternalInput").ap()
    dzl = nc.dram_tensor("zl", [128, 896], BF16, kind="ExternalInput").ap()
    dy = nc.dram_tensor("y", [NSEQ, T, cfg.D], F32, kind="ExternalOutput").ap()
    drp = nc.dram_tensor("rp_scratch", [128, 64 * 256], BF16, kind="Internal").ap()
    cw1 = nc.dram_tensor("cw1", [4, 128, 2 * 128 * 32], BF16, kind="Internal").ap()
    cw2 = nc.dram_tensor("cw2", [4, 128, 32 * 256], BF16, kind="Internal").ap()
    cwo = nc.dram_tensor("cwo", [128, CH * 1024], BF16, kind="Internal").ap()
    cwkv = nc.dram_tensor("cwkv", [128, CH * 256], BF16, kind="Internal").ap()
    ctie = nc.dram_tensor("ctie", [1, T], BF16, kind="Internal").ap()

    def at(name, shape, dtype, off):
        return nc.alloc_sbuf_tensor_at(name, shape, dtype, offset=off).ap()

    o = [17 * 1024]
    def bump(n):
        r = o[0]; o[0] = (o[0] + n + 31) & ~31
        return r

    c_pe = at("c_pe", [128, T], F32, bump(T * 4))
    c_ident = at("c_ident", [128, 128], BF16, bump(256))
    c_zl = at("c_zl", [128, 896], BF16, bump(896 * 2))
    c_bq = at("c_bq", [128, CH], F32, bump(CH * 4))
    c_bk = at("c_bk", [128, 1], F32, bump(4))
    c_bvb = at("c_bvb", [128, 128], F32, bump(512))
    c_bc = at("c_bc", [128, 1], F32, bump(4))
    c_b1 = at("c_b1", [128, 2048], BF16, bump(2048 * 2))
    c_b2 = at("c_b2", [128, 1024], BF16, bump(2048))
    c_ones1 = at("c_ones1", [1, 128], BF16, bump(256))
    c_eps = at("c_eps", [128, 1], F32, bump(4))
    c_negC = at("c_negC", [128, 1], F32, bump(4))
    w_c = at("w_c", [128, CH * 128], BF16, bump(CH * 128 * 2))
    st_m = at("st_m", [128, NT], F32, bump(NT * 4))
    st_v = at("st_v", [128, NT], F32, bump(NT * 4))
    st_r = at("st_r", [128, NT], F32, bump(NT * 4))
    st_b = at("st_b", [128, NT], F32, bump(NT * 4))
    rs = at("rs", [128, 4], F32, bump(16))
    gmax = at("gmax", [128, NG], F32, bump(NG * 4))

    w_q = at("w_q", [128, CH * 1024], BF16, bump(CH * 1024 * 2))
    r1 = bump(NT * 1024 * 4)
    x2 = at("x2", [128, NT * 1024], F32, r1)

    sz_r2 = max(CH * T * 2, CH * 1024 * 2, 2 * NG * 128 * 2, NG * 256 * 2)
    r2 = bump(sz_r2)
    xT = at("xT", [128, CH * T], BF16, r2)
    x2t_stage = at("x2t", [128, 8 * 512], BF16, r2)
    c_tie = at("c_tie", [1, T], BF16, r2 + 16384)
    w_o = at("w_o", [128, CH * 1024], BF16, r2)
    gq = at("gq", [128, 2 * NG * 128], BF16, r2)
    rpe = at("rpe", [128, NG * 256], BF16, r2)

    sz_r3 = max(CH * T * 2, 2 * (2 * 128 * 32 * 2), 64 * 256 * 2)
    r3 = bump(sz_r3)
    qT = at("qT", [128, CH * T], BF16, r3)
    w_1 = [at(f"w_1_{i}", [128, 2 * 128 * 32], BF16, r3 + i * 2 * 128 * 32 * 2)
           for i in range(2)]

    rpn = at("rpn", [128, 64 * 256], BF16, r3)

    sz_attn = T * 2 + NT * 132 * 2 + 3 * 1024 + H * 512 * 2
    sz_r4 = max(sz_attn, 32 * 256 * 2)
    r4 = bump(sz_r4)
    kT = at("kT", [128, T], BF16, r4)
    vS = at("vS", [128, NT * 132], BF16, r4 + T * 2)
    pT = [at(f"pT{i}", [128, 512], BF16, r4 + T * 2 + NT * 132 * 2 + i * 1024)
          for i in range(2)]
    pTn = at("pTn", [128, 512], BF16, r4 + T * 2 + NT * 132 * 2 + 2 * 1024)
    atT = at("atT", [128, H * 512], BF16, r4 + T * 2 + NT * 132 * 2 + 3 * 1024)
    w_2t = [at("w_2t_0", [128, 32 * 256], BF16, r4)] * 2

    r5 = bump(max(2 * 1024 * 4, 2048 * 4))
    xtile = [at(f"xtile{i}", [128, 1024], F32, r5 + i * 4096) for i in range(2)]
    w_kv = at("w_kv", [128, CH * 256], BF16, r5)
    xtile_bf = at("xtile_bf", [128, 4096], BF16, r5)
    e1p = at("e1p", [128, 2048], F32, r5)

    r6 = bump(max(T * 4, 2048 * 2 + 2 * 512))
    lg = at("lg", [128, T], F32, r6)
    inr = at("inr", [128, 2048], BF16, r6)
    x2b = [at(f"x2b{i}", [128, 256], BF16, r6 + 2048 * 2 + i * 512) for i in range(2)]

    r7 = bump(T * 2 + NT * 128 * 2)
    pmT = at("pmT", [128, T], BF16, r7)
    pmt = at("pmt", [128, NT * 128], BF16, r7 + T * 2)

    assert o[0] <= 224 * KB - 256, f"SBUF over: {o[0]}"

    pp_s = [nc.alloc_psum_tensor(f"pp_s{i}", [128, 512], F32).ap() for i in range(2)]
    pp_pv = [nc.alloc_psum_tensor(f"pp_pv{i}", [128, 512], F32).ap() for i in range(2)]
    pp_tr = nc.alloc_psum_tensor("pp_tr", [128, 512], BF16).ap()
    pp_tr2 = nc.alloc_psum_tensor("pp_tr2", [128, 512], BF16).ap()
    pp_w = nc.alloc_psum_tensor("pp_w", [128, 1024], F32).ap()

    semnames = ["sPE", "sACT", "sDVE", "sLD", "sST", "sRS", "sGW",
                "sXT0", "sXT1", "sW10", "sW11", "sW20", "sW21",
                "sWA", "sWB", "sWC", "sWD", "sWE", "sWS", "sS1a", "sS1b", "sS2", "sS3", "sRSa", "sRSb"]
    sems, cms = {}, []
    for n in semnames:
        cm = nc.semaphore(n); sems[n] = cm.__enter__(); cms.append(cm)

    cnt = defaultdict(int)
    ops = {e: [] for e in ["sync", "gpsimd", "tensor", "vector", "scalar"]}
    guards = {}

    def emit(eng, fn): ops[eng].append(fn)

    def inc(eng, sem, n, buildfn):
        cnt[sem] += n; v = cnt[sem]
        emit(eng, lambda e, b=buildfn, s=sem, n=n: b(e).then_inc(sems[s], n))
        return (sem, v)

    def wait(eng, mark):
        if mark is None: return
        sem, v = mark
        emit(eng, lambda e, s=sem, v=v: e.wait_ge(sems[s], v))

    def now(sem): return (sem, cnt[sem])

    def gwait(eng, key):
        m = guards.pop(key, None)
        if m is not None: wait(eng, m)

    def gset(key, mark): guards[key] = mark

    for dst, src in [(c_pe, dpehi), (c_ident, dident), (c_zl, dzl), (c_bq, dbq),
                     (c_bk, dbk), (c_bvb, dbvb), (c_bc, dbc),
                     (c_ones1, dones1)]:
        inc("sync", "sLD", 16, lambda e, d=dst, s=src: e.dma_start(out=d, in_=s))
    inc("sync", "sLD", 16, lambda e: e.dma_start(out=xtile[0][:, 0:T] if T <= 1024 else
                                                 xtile[0][:, :], in_=dpelo[:, 0:min(T, 1024)]))
    pe_lo_extra = None
    if T > 1024:
        pe_lo_extra = inc("sync", "sLD", 16, lambda e: e.dma_start(
            out=xtile[1][:, 0:T - 1024], in_=dpelo[:, 1024:T]))
    inc("gpsimd", "sGW", 16, lambda e: e.dma_start(out=c_b1, in_=db1))
    inc("gpsimd", "sGW", 16, lambda e: e.dma_start(out=c_b2, in_=db2))
    inc("gpsimd", "sGW", 16, lambda e: e.dma_start(out=w_c, in_=dwc))
    inc("gpsimd", "sGW", 16, lambda e: e.dma_start(out=w_q, in_=dwq))
    ld_all = now("sLD")
    wait("vector", ld_all)
    m0 = inc("vector", "sDVE", 1, lambda e: e.tensor_tensor(
        c_pe[:, 0:min(T, 1024)], c_pe[:, 0:min(T, 1024)],
        xtile[0][:, 0:min(T, 1024)], ALU.add))
    gset(("xt", 0), m0)
    if T > 1024:
        m1 = inc("vector", "sDVE", 1, lambda e: e.tensor_tensor(
            c_pe[:, 1024:T], c_pe[:, 1024:T], xtile[1][:, 0:T - 1024], ALU.add))
        gset(("xt", 1), m1)
    inc("vector", "sDVE", 1, lambda e: e.memset(c_eps[:, :], 1e-5))
    inc("vector", "sDVE", 1, lambda e: e.memset(c_negC[:, :], -cfg.CSHIFT))
    ones_mark = inc("vector", "sDVE", 1, lambda e: e.memset(
        bass.AP(vS.tensor, vS.offset + 128, [[vS.ap[0][0], 128], [132, NT]]), 1.0))
    consts_v = now("sDVE"); consts_g = now("sGW")
    for eng in ["tensor", "scalar"]:
        wait(eng, consts_v); wait(eng, ld_all); wait(eng, consts_g)
    wait("vector", consts_g)

    XTS = ["sXT0", "sXT1"]; W1S = ["sW10", "sW11"]; W2S = ["sW20", "sW21"]
    bkeys4g = [("ps", 0), ("ps", 1), ("pv", 0), ("pv", 1)]
    banks4g = pp_s + pp_pv
    prev_relu = None
    prev_stores = None
    st_all_ws = None
    st_wkv = None
    st_s2_all = None; st_s1a = None; st_s1b = None

    def emit_ln_stats_finish():
        wait("vector", now("sACT"))
        wait("vector", now("sDVE"))
        inc("vector", "sDVE", 1, lambda e: e.tensor_tensor(st_v, st_v, st_r, ALU.add))
        inc("vector", "sDVE", 1, lambda e: e.tensor_scalar(st_m, st_m, 1.0 / 1024, None, ALU.mult))
        wait("vector", now("sDVE"))
        inc("vector", "sDVE", 1, lambda e: e.tensor_scalar(st_v, st_v, 1.0 / 1024, None, ALU.mult))
        inc("vector", "sDVE", 1, lambda e: e.tensor_tensor(st_b, st_m, st_m, ALU.mult))
        wait("vector", now("sDVE"))
        lv = inc("vector", "sDVE", 1, lambda e: e.tensor_tensor(st_v, st_v, st_b, ALU.subtract))
        wait("scalar", lv)
        wait("scalar", now("sACT"))
        sv2 = inc("scalar", "sACT", 1, lambda e: e.activation(out=st_v, in_=st_v,
                                                              func=AF.Sqrt, bias=c_eps))
        wait("vector", sv2)
        inc("vector", "sDVE", 1, lambda e: e.reciprocal(st_r, st_v))
        wait("vector", now("sDVE"))
        inc("vector", "sDVE", 1, lambda e: e.tensor_tensor(st_b, st_m, st_r, ALU.mult))
        wait("vector", now("sDVE"))
        bv = inc("vector", "sDVE", 1, lambda e: e.tensor_scalar(st_b, st_b, -1.0, None, ALU.mult))
        return bv

    def emit_ln(p2_act):
        bv = emit_ln_stats_finish()
        wait("scalar", bv)
        wait("scalar", now("sACT"))
        ln1 = None
        for tt in range(NT):
            ln1 = inc("scalar", "sACT", 1, lambda e, tt=tt: e.activation(
                out=x2[:, tt * 1024: tt * 1024 + 1024],
                in_=x2[:, tt * 1024: tt * 1024 + 1024],
                func=AF.Identity, scale=st_r[:, tt: tt + 1], bias=st_b[:, tt: tt + 1]))
        return ln1

    def emit_ln2_finish():
        mred = now("sDVE")
        wait("scalar", mred)
        wait("scalar", now("sACT"))
        gwait("scalar", ("pT", 0)); gwait("scalar", ("pT", 1))
        for tt in range(NT):
            if tt > 0:
                wait("scalar", now("sACT"))
            inc("scalar", "sACT", 1, lambda e, tt=tt: e.activation(
                out=pT[0][:, :], in_=x2[:, tt * 1024: tt * 1024 + 512],
                func=AF.Square, accum_out=st_v[:, tt: tt + 1]))
            inc("scalar", "sACT", 1, lambda e, tt=tt: e.activation(
                out=pT[1][:, :], in_=x2[:, tt * 1024 + 512: tt * 1024 + 1024],
                func=AF.Square, accum_out=st_r[:, tt: tt + 1]))
        sq2 = now("sACT")
        gset(("pT", 0), sq2); gset(("pT", 1), sq2)
        bv = emit_ln_stats_finish()
        wait("scalar", bv)
        wait("scalar", now("sACT"))
        return bv

    for s in range(NSEQ):
        gwait("gpsimd", "R2xT")
        wait("gpsimd", now("sGW"))
        if prev_relu is not None:
            wait("gpsimd", prev_relu)
        if s == 0:
            wait("gpsimd", consts_v)
        mk_xT = inc("gpsimd", "sGW", 16, lambda e, s=s: e.dma_start(
            out=xT.rearrange("p (c t) -> p c t", c=CH),
            in_=dxT[s].rearrange("(c p) t -> p c t", p=128)))
        if s == 0:
            mk_wkv = inc("gpsimd", "sGW", 16, lambda e: e.dma_start(out=w_kv, in_=dwkv))
            wait("sync", mk_wkv)
            st_wkv = inc("sync", "sS3", 16, lambda e: e.dma_start(out=cwkv, in_=w_kv))
        else:
            wait("sync", st_all_ws)
            wait("sync", st_wkv)
            mk_wkv = inc("sync", "sWE", 16, lambda e: e.dma_start(out=w_kv, in_=cwkv))

        if s == 1:
            wait("vector", st_s2_all); wait("scalar", st_s2_all)
        wait("tensor", mk_wkv)
        if s > 0:
            wait("tensor", mk_xT)
        gwait("tensor", "R3qT")
        gwait("tensor", "R4kv")
        for hc in range(H):
            for tb in range(NQB):
                bk_ = ("ps", tb % 2); ps = pp_s[tb % 2]
                gwait("tensor", bk_)
                mmq = None
                for c in range(CH):
                    mmq = inc("tensor", "sPE", 1, lambda e, ps=ps, c=c, hc=hc, tb=tb: e.matmul(
                        ps[:, :], w_q[:, c * 1024 + hc * 128: c * 1024 + hc * 128 + 128],
                        xT[:, c * T + tb * 512: c * T + tb * 512 + 512],
                        start=(c == 0), stop=(c == CH - 1)))
                wait("vector", mmq)
                ev = inc("vector", "sDVE", 1, lambda e, ps=ps, hc=hc, tb=tb: e.scalar_tensor_tensor(
                    qT[:, hc * T + tb * 512: hc * T + tb * 512 + 512], ps[:, :],
                    c_bq[:, hc: hc + 1], c_pe[:, tb * 512: tb * 512 + 512], ALU.add, ALU.add))
                gset(bk_, ev)
        for tb in range(NQB):
            bk_ = ("ps", tb % 2); ps = pp_s[tb % 2]
            gwait("tensor", bk_)
            mmk = None
            for c in range(CH):
                mmk = inc("tensor", "sPE", 1, lambda e, ps=ps, c=c, tb=tb: e.matmul(
                    ps[:, :], w_kv[:, c * 256: c * 256 + 128],
                    xT[:, c * T + tb * 512: c * T + tb * 512 + 512],
                    start=(c == 0), stop=(c == CH - 1)))
            wait("vector", mmk)
            ev = inc("vector", "sDVE", 1, lambda e, ps=ps, tb=tb: e.scalar_tensor_tensor(
                kT[:, tb * 512: tb * 512 + 512], ps[:, :], c_bk,
                c_pe[:, tb * 512: tb * 512 + 512], ALU.add, ALU.add))
            gset(bk_, ev)
        for tt in range(NT):
            bk_ = ("pv", tt % 2); ps = pp_pv[tt % 2]
            gwait("tensor", bk_)
            mmv = None
            for c in range(CH):
                mmv = inc("tensor", "sPE", 1, lambda e, ps=ps, c=c, tt=tt: e.matmul(
                    ps[:, 0:128], xT[:, c * T + tt * 128: c * T + tt * 128 + 128],
                    w_kv[:, c * 256 + 128: c * 256 + 256],
                    start=(c == 0), stop=(c == CH - 1)))
            wait("vector", mmv)
            if tt == 0:
                if s == 0:
                    wait("vector", ones_mark)
                inc("vector", "sDVE", 1, lambda e: e.memset(
                    bass.AP(vS.tensor, vS.offset + 128,
                            [[vS.ap[0][0], 128], [132, NT]]), 1.0))
            ev = inc("vector", "sDVE", 1, lambda e, ps=ps, tt=tt: e.tensor_tensor(
                vS[:, tt * 132: tt * 132 + 128], ps[:, 0:128], c_bvb, ALU.add))
            gset(bk_, ev)
        p1_done = now("sPE")
        if s == 0:
            wait("gpsimd", p1_done)
            mk_wo = inc("gpsimd", "sGW", 16, lambda e: e.dma_start(out=w_o, in_=dwo))
            mk_tie0 = inc("gpsimd", "sGW", 16, lambda e: e.dma_start(out=c_tie, in_=dtie))
            wait("sync", mk_tie0)
            inc("sync", "sWS", 16, lambda e: e.dma_start(out=cwo, in_=w_o))
            st_tie = inc("sync", "sWS", 16, lambda e: e.dma_start(out=ctie, in_=c_tie))
            mk_tie = mk_tie0
        else:
            wait("sync", p1_done)
            mk_wo = inc("sync", "sWE", 16, lambda e: e.dma_start(out=w_o, in_=cwo))
            mk_tie = inc("sync", "sWE", 16, lambda e: e.dma_start(out=c_tie, in_=ctie))

        for qb in range(NQB):
            epi = {"pv_last": None}
            def emit_scores(h, kc, exp_m):
                ping = kc % 2
                gwait("tensor", ("ps", ping))
                if kc >= 2:
                    wait("tensor", exp_m[kc - 2])
                return inc("tensor", "sPE", 1, lambda e, kc=kc, ping=ping, h=h, qb=qb: e.matmul(
                    pp_s[ping][:, :], kT[:, kc * 128: kc * 128 + 128],
                    qT[:, h * T + qb * 512: h * T + qb * 512 + 512],
                    start=True, stop=True))
            def emit_exp(h, kc, sv, exp_m, mask_m):
                ping = kc % 2
                wait("scalar", sv)
                gwait("scalar", ("pT", ping))
                ev = inc("scalar", "sACT", 1, lambda e, ping=ping: e.activation(
                    out=pT[ping][:, :], in_=pp_s[ping][:, :], func=AF.Exp,
                    bias=c_negC, scale=cfg.SCALE))
                exp_m[kc] = ev
                gset(("ps", ping), ev)
                j = kc - 4 * qb
                if j >= 0:
                    wait("vector", ev)
                    mv = inc("vector", "sDVE", 1, lambda e, ping=ping, j=j: e.tensor_tensor(
                        pT[ping][:, :], pT[ping][:, :],
                        c_zl[:, (3 - j) * 128: (3 - j) * 128 + 512], ALU.mult))
                    mask_m[kc] = mv
            def emit_pv(h, kc, kcmax, exp_m, mask_m):
                ping = kc % 2
                if kc in mask_m:
                    wait("tensor", mask_m[kc])
                else:
                    wait("tensor", exp_m[kc])
                if kc == 0:
                    gwait("tensor", ("pv", 0)); gwait("tensor", ("pv", 1))
                pvl = None
                for sub in range(4):
                    bank = pp_pv[sub // 2]; colo = (sub % 2) * 129
                    pvl = inc("tensor", "sPE", 1, lambda e, bank=bank, colo=colo,
                              ping=ping, sub=sub, kc=kc, kcmax=kcmax: e.matmul(
                        bank[:, colo: colo + 129],
                        pT[ping][:, sub * 128: sub * 128 + 128],
                        vS[:, kc * 132: kc * 132 + 129],
                        start=(kc == 0 and sub % 2 == 0), stop=(kc == kcmax - 1),
                        skip_group_check=True))
                gset(("pT", ping), pvl)
                return pvl
            def emit_epilogue_pre(h):
                wait("vector", epi["pv_last"])
                gwait("vector", ("rs",))
                rv = None
                for sub in range(4):
                    bank = pp_pv[sub // 2]; colo = (sub % 2) * 129
                    rv = inc("vector", "sDVE", 1, lambda e, bank=bank, colo=colo, sub=sub:
                             e.reciprocal(rs[:, sub: sub + 1], bank[:, colo + 128: colo + 129]))
                wait("scalar", rv)
                gwait("scalar", ("pTn",))
                nm = None
                for sub in range(4):
                    bank = pp_pv[sub // 2]; colo = (sub % 2) * 129
                    nm = inc("scalar", "sACT", 1, lambda e, bank=bank, colo=colo, sub=sub:
                             e.activation(out=pTn[:, sub * 128: sub * 128 + 128],
                                          in_=bank[:, colo: colo + 128],
                                          func=AF.Identity, scale=rs[:, sub: sub + 1]))
                gset(("pv", 0), nm); gset(("pv", 1), nm)
                gset(("rs",), nm)
                epi["nm"] = nm
            def emit_epilogue_tensor(h):
                wait("tensor", epi["nm"])
                gwait("tensor", ("tr", 0))
                tv = None
                for sub in range(4):
                    tv = inc("tensor", "sPE", 1, lambda e, sub=sub: e.transpose(
                        pp_tr[:, sub * 128: sub * 128 + 128],
                        pTn[:, sub * 128: sub * 128 + 128], c_ident))
                gset(("pTn",), tv)
                wait("vector", tv)
                av = inc("vector", "sDVE", 1, lambda e, h=h: e.tensor_copy(
                    atT[:, h * 512: h * 512 + 512], pp_tr[:, :]))
                gset(("tr", 0), av)
                epi["av"] = av
            for h in range(H):
                kcmax = (qb + 1) * 4
                exp_m = {}; mask_m = {}
                sv = emit_scores(h, 0, exp_m)
                emit_exp(h, 0, sv, exp_m, mask_m)
                if kcmax > 1:
                    sv = emit_scores(h, 1, exp_m)
                    emit_exp(h, 1, sv, exp_m, mask_m)
                if h > 0:
                    emit_epilogue_tensor(h - 1)
                for kc in range(kcmax):
                    nxt = kc + 2
                    sv = None
                    if nxt < kcmax:
                        sv = emit_scores(h, nxt, exp_m)
                    epi["pv_last"] = emit_pv(h, kc, kcmax, exp_m, mask_m)
                    if sv is not None:
                        emit_exp(h, nxt, sv, exp_m, mask_m)
                emit_epilogue_pre(h)
            emit_epilogue_tensor(H - 1)
            av = epi["av"]
            wait("tensor", mk_tie)
            wait("tensor", av)
            for t4 in range(4):
                tt = qb * 4 + t4
                xs = XTS[tt % 2]
                if qb == 0 and t4 == 0:
                    wait("sync", p1_done)
                    if s == 0:
                        wait("sync", st_wkv)
                gwait("sync", ("xt", tt % 2))
                xv = inc("sync", xs, 16, lambda e, s=s, tt=tt: e.dma_start(
                    out=xtile[tt % 2], in_=dx[s, tt * 128: tt * 128 + 128, :]))
                gwait("tensor", ("pw", 0)); gwait("tensor", ("pw", 1))
                wv = None
                for h in range(H):
                    for half in range(2):
                        wv = inc("tensor", "sPE", 1, lambda e, h=h, half=half, t4=t4: e.matmul(
                            pp_w[:, half * 512: half * 512 + 512],
                            atT[:, h * 512 + t4 * 128: h * 512 + t4 * 128 + 128],
                            w_o[:, h * 1024 + half * 512: h * 1024 + half * 512 + 512],
                            start=(h == 0), stop=(h == H - 1), skip_group_check=True))
                wait("vector", wv); wait("vector", xv)
                if qb == 0 and t4 == 0 and s > 0:
                    wait("vector", prev_stores)
                rv2 = inc("vector", "sDVE", 1, lambda e, tt=tt: e.tensor_tensor(
                    x2[:, tt * 1024: tt * 1024 + 1024], pp_w[:, :],
                    xtile[tt % 2][:, :], ALU.add))
                gset(("pw", 0), rv2); gset(("pw", 1), rv2)
                gset(("xt", tt % 2), rv2)
                wait("vector", rv2)
                inc("vector", "sDVE", 1, lambda e, tt=tt: e.tensor_reduce(
                    st_m[:, tt: tt + 1], x2[:, tt * 1024: tt * 1024 + 1024], AX.X, ALU.add))
                wait("scalar", rv2)
                gwait("scalar", ("pT", 0)); gwait("scalar", ("pT", 1))
                sq = inc("scalar", "sACT", 1, lambda e, tt=tt: e.activation(
                    out=pT[0][:, :], in_=x2[:, tt * 1024: tt * 1024 + 512],
                    func=AF.Square, accum_out=st_v[:, tt: tt + 1]))
                sq = inc("scalar", "sACT", 1, lambda e, tt=tt: e.activation(
                    out=pT[1][:, :], in_=x2[:, tt * 1024 + 512: tt * 1024 + 1024],
                    func=AF.Square, accum_out=st_r[:, tt: tt + 1]))
                gset(("pT", 0), sq); gset(("pT", 1), sq)
        ln1 = emit_ln(now("sACT"))

        wait("gpsimd", ln1)
        w1sems = [["sW10", "sW11"], ["sWA", "sWB"]]
        w1_store = {}
        def w1_load(q4):
            if s == 0:
                if q4 >= 2:
                    wait("gpsimd", w1_store[q4 - 2])
                m = inc("gpsimd", w1sems[0][q4 % 2], 16,
                        lambda e, q=q4: e.dma_start(out=w_1[q % 2], in_=dw1[q]))
                wait("sync", m)
                w1_store[q4] = inc("sync", ["sS1a", "sS1b"][q4 % 2], 16,
                                   lambda e, q=q4: e.dma_start(out=cw1[q], in_=w_1[q % 2]))
                return m
            return inc("scalar", w1sems[1][q4 % 2], 16,
                       lambda e, q=q4: e.dma_start(out=w_1[q % 2], in_=cw1[q]))
        if s == 1:
            wait("scalar", st_all_ws)
            wait("scalar", st_s2_all)
            wait("scalar", st_s1a); wait("scalar", st_s1b)
        gw1 = {0: w1_load(0)}
        def w2t_load(q4, part):
            c0 = 0 if part == 0 else 2048
            if s == 0:
                sem = "sW20" if part == 0 else "sW21"
                inc("gpsimd", sem, 16, lambda e, q4=q4, c0=c0: e.dma_start(
                    out=w_2t[0][:, c0: c0 + 2048], in_=dw2[q4][:, c0: c0 + 2048]))
                m = inc("gpsimd", sem, 16, lambda e, q4=q4, c0=c0: e.dma_start(
                    out=w_2t[0][:, 4096 + c0: 4096 + c0 + 2048],
                    in_=dw2[q4][:, 4096 + c0: 4096 + c0 + 2048]))
                wait("sync", m)
                inc("sync", "sS2", 16, lambda e, q4=q4, c0=c0: e.dma_start(
                    out=cw2[q4][:, c0: c0 + 2048], in_=w_2t[0][:, c0: c0 + 2048]))
                w2t_store[q4] = inc("sync", "sS2", 16, lambda e, q4=q4, c0=c0: e.dma_start(
                    out=cw2[q4][:, 4096 + c0: 4096 + c0 + 2048],
                    in_=w_2t[0][:, 4096 + c0: 4096 + c0 + 2048]))
                return m
            sem = "sWC" if part == 0 else "sWD"
            inc("scalar", sem, 16, lambda e, q4=q4, c0=c0: e.dma_start(
                out=w_2t[0][:, c0: c0 + 2048], in_=cw2[q4][:, c0: c0 + 2048]))
            return inc("scalar", sem, 16, lambda e, q4=q4, c0=c0: e.dma_start(
                out=w_2t[0][:, 4096 + c0: 4096 + c0 + 2048],
                in_=cw2[q4][:, 4096 + c0: 4096 + c0 + 2048]))
        w2t_store = {}
        gw2 = {0: w2t_load(0, 0)}
        gw2b = {0: w2t_load(0, 1)}
        wait("scalar", ln1)
        wait("tensor", mk_tie)
        if s == 0:
            wait("vector", st_tie)
            wait("scalar", st_wkv)
        trb = [pp_tr, pp_tr2]
        cast_m = {}
        def emit_cast_full(tb):
            gwait("scalar", ("xtbf",))
            cm = None
            for t4 in range(4):
                tt = tb * 4 + t4
                cm = inc("scalar", "sACT", 1, lambda e, tt=tt, t4=t4: e.activation(
                    out=xtile_bf[:, t4 * 1024: t4 * 1024 + 1024],
                    in_=x2[:, tt * 1024: tt * 1024 + 1024], func=AF.Copy))
            cast_m[tb] = cm
        for tb in range(NQB):
            emit_cast_full(tb)
            wait("tensor", cast_m[tb])
            gwait("tensor", ("x2t",))
            cvs = {}
            for c in range(CH):
                bnk = trb[c % 2]
                gwait("tensor", ("tr", c % 2))
                tvv = None
                for t4 in range(4):
                    tvv = inc("tensor", "sPE", 1, lambda e, c=c, t4=t4, bnk=bnk: e.transpose(
                        bnk[:, t4 * 128: t4 * 128 + 128],
                        xtile_bf[:, t4 * 1024 + c * 128: t4 * 1024 + c * 128 + 128],
                        c_ident))
                wait("vector", tvv)
                cv = inc("vector", "sDVE", 1, lambda e, c=c, bnk=bnk: e.tensor_copy(
                    x2t_stage[:, c * 512: c * 512 + 512], bnk[:, :]))
                gset(("tr", c % 2), cv)
                cvs[c] = cv
            gset(("xtbf",), cvs[CH - 1])
            gwait("tensor", ("pw", 0))
            lgv = None
            for c in range(CH):
                wait("tensor", cvs[c])
                lgv = inc("tensor", "sPE", 1, lambda e, c=c: e.matmul(
                    pp_w[:, 0:512], w_c[:, c * 128: c * 128 + 128],
                    x2t_stage[:, c * 512: c * 512 + 512], start=(c == 0), stop=False))
            lgv = inc("tensor", "sPE", 1, lambda e, tb=tb: e.matmul(
                pp_w[:, 0:512], c_ones1, c_tie[:, tb * 512: tb * 512 + 512],
                start=False, stop=True))
            gset(("x2t",), lgv)
            wait("vector", lgv)
            lev = inc("vector", "sDVE", 1, lambda e, tb=tb: e.tensor_scalar(
                lg[:, tb * 512: tb * 512 + 512], pp_w[:, 0:512], c_bc, None, ALU.add))
            gset(("pw", 0), lev)
        wait("vector", now("sDVE"))
        for g in range(NG):
            inc("vector", "sDVE", 1, lambda e, g=g: e.tensor_reduce(
                gmax[:, g: g + 1], lg[:, g * 32: g * 32 + 32], AX.X, ALU.max))
        gv = now("sDVE"); wait("vector", gv)
        pv2 = None
        for g in range(NG):
            pv2 = inc("vector", "sDVE", 1, lambda e, g=g: e.tensor_single_scalar(
                pmT[:, g * 32: g * 32 + 32], lg[:, g * 32: g * 32 + 32],
                gmax[:, g: g + 1], ALU.is_equal))
        wait("tensor", pv2)
        pmv = None
        for tt in range(NT):
            bnk = trb[tt % 2]
            gwait("tensor", ("tr", tt % 2))
            tv2 = inc("tensor", "sPE", 1, lambda e, tt=tt, bnk=bnk: e.transpose(
                bnk[:, 0:128], pmT[:, tt * 128: tt * 128 + 128], c_ident))
            wait("vector", tv2)
            pmv = inc("vector", "sDVE", 1, lambda e, tt=tt, bnk=bnk: e.tensor_copy(
                pmt[:, tt * 128: tt * 128 + 128], bnk[:, 0:128]))
            gset(("tr", tt % 2), pmv)
        e1p_marks = {}
        for q4 in range(4):
            wait("tensor", gw1[q4])
            if q4 + 1 < 4:
                if q4 >= 1:
                    wait("gpsimd" if s == 0 else "scalar", now("sPE"))
                gw1[q4 + 1] = w1_load(q4 + 1)
            if q4 == 0:
                wait("tensor", pmv)
            gev = {}
            for tt in range(NT):
                if q4 == 0 and tt == 0:
                    wait("scalar", pv2)
                    if s == 0:
                        wait("vector", st_tie); wait("scalar", st_tie)
                gwait("scalar", ("x2b", tt % 2))
                cm2 = inc("scalar", "sACT", 1, lambda e, tt=tt, q4=q4: e.activation(
                    out=x2b[tt % 2], in_=x2[:, tt * 1024 + q4 * 256: tt * 1024 + q4 * 256 + 256],
                    func=AF.Copy))
                wait("tensor", cm2)
                lastg = None
                for j in range(4):
                    g = tt * 4 + j
                    bidx = g % 4
                    gwait("tensor", bkeys4g[bidx])
                    gvv = None
                    for cc in range(2):
                        gvv = inc("tensor", "sPE", 1, lambda e, j=j, cc=cc, tt=tt, bidx=bidx: e.matmul(
                            banks4g[bidx][:, cc * 128: cc * 128 + 128],
                            x2b[tt % 2][j * 32: j * 32 + 32, cc * 128: cc * 128 + 128],
                            pmt[j * 32: j * 32 + 32, tt * 128: tt * 128 + 128],
                            start=(cc == 0), stop=True,
                            tile_position=(32 * j, 0), skip_group_check=True))
                    lastg = gvv
                    eng2 = "vector" if (g % 2 == 0) else "scalar"
                    wait(eng2, gvv)
                    sem2 = "sDVE" if eng2 == "vector" else "sACT"
                    cp2 = (lambda e: e.tensor_copy) if eng2 == "vector" else (lambda e: e.copy)
                    m = inc(eng2, sem2, 1, lambda e, g=g, bidx=bidx, cp2=cp2: cp2(e)(
                        bass.AP(gq.tensor, gq.offset + g * 128,
                                [[gq.ap[0][0], 128], [NG * 128, 2], [1, 128]]),
                        banks4g[bidx][:, 0:256].rearrange("p (a b) -> p a b", b=128)))
                    gev[g] = m
                    gset(bkeys4g[bidx], m)
                gset(("x2b", tt % 2), lastg)
            for r in range(4):
                bk_ = ("ps", r % 2); ps = pp_s[r % 2]
                gwait("tensor", bk_)
                e1v = None
                for qq in range(8):
                    qidx = r * 8 + qq
                    for b in range(4):
                        es = qidx * 4 + b
                        if qq == 0 and b == 0:
                            wait("tensor", gev[NG - 1])
                            wait("tensor", gev[NG - 2])
                        for cc in range(2):
                            e1v = inc("tensor", "sPE", 1, lambda e, ps=ps, b=b, qq=qq, es=es,
                                      cc=cc, q4=q4: e.matmul(
                                ps[b * 32: b * 32 + 32, qq * 64: qq * 64 + 64],
                                w_1[q4 % 2][:, cc * 4096 + es * 32: cc * 4096 + es * 32 + 32],
                                bass.AP(gq.tensor, gq.offset + cc * NG * 128 + es,
                                        [[gq.ap[0][0], 128], [128, 64]]),
                                start=(cc == 0), stop=(cc == 1),
                                tile_position=(0, 32 * b), skip_group_check=True))
                wait("vector", e1v)
                if q4 > 0:
                    wait("vector", e1p_marks[r])
                if q4 == 0:
                    m = inc("vector", "sDVE", 1, lambda e, r=r, ps=ps: e.scalar_tensor_tensor(
                        e1p[:, r * 512: r * 512 + 512], ps[:, :], 0.0,
                        c_b1[:, r * 512: r * 512 + 512], ALU.add, ALU.add))
                else:
                    m = inc("vector", "sDVE", 1, lambda e, r=r, ps=ps: e.tensor_tensor(
                        e1p[:, r * 512: r * 512 + 512], e1p[:, r * 512: r * 512 + 512],
                        ps[:, :], ALU.add))
                e1p_marks[r] = m
                gset(bk_, m)
                if q4 == 3:
                    wait("scalar", m)
                    inc("scalar", "sACT", 1, lambda e, r=r: e.activation(
                        out=inr[:, r * 512: r * 512 + 512],
                        in_=e1p[:, r * 512: r * 512 + 512], func=AF.Relu))
        relu_done = now("sACT")

        banks4 = [pp_s[0], pp_s[1], pp_pv[0], pp_pv[1]]
        bkeys4 = [("ps", 0), ("ps", 1), ("pv", 0), ("pv", 1)]
        sc_last = None
        rs_marks = {}
        rsq_all = {}

        def emit_e2(q4):
            wait("tensor", gw2[q4])
            if q4 == 0:
                wait("tensor", relu_done)
                if s == 0:
                    for eng_ in ("vector", "scalar"):
                        wait(eng_, w1_store[2]); wait(eng_, w1_store[3])
            rsq_marks = []
            rpn_guard = guards.pop(("rpn",), None)
            for r in range(64):
                bidx = r % 4; ps = banks4[bidx]
                gwait("tensor", bkeys4[bidx])
                if r == 32:
                    wait("tensor", gw2b[q4])
                e2v = None
                for half in range(2):
                    es = r + 64 * half
                    b = es % 4; q = es // 4
                    e2v = inc("tensor", "sPE", 1, lambda e, ps=ps, b=b, q=q, half=half,
                              q4=q4: e.matmul(
                        ps[half * 64: half * 64 + 64, 0:256],
                        inr[b * 32: b * 32 + 32, q * 64: q * 64 + 64],
                        w_2t[0][b * 32: b * 32 + 32, q * 256: q * 256 + 256],
                        start=True, stop=True,
                        tile_position=(b * 32, 64 * half),
                        skip_group_check=True))
                eng2 = "vector" if (r % 2 == 0) else "scalar"
                sem2 = "sDVE" if eng2 == "vector" else "sACT"
                wait(eng2, e2v)
                if rpn_guard is not None:
                    wait(eng2, rpn_guard)
                cp3 = (lambda e: e.tensor_copy) if eng2 == "vector" else (lambda e: e.copy)
                m = inc(eng2, sem2, 1, lambda e, r=r, ps=ps, cp3=cp3: cp3(e)(
                    rpn[:, r * 256: r * 256 + 256], ps[:, 0:256]))
                gset(bkeys4[bidx], m)
                rsq_marks.append(m)
            if q4 + 1 < 4:
                eng_w = "gpsimd" if s == 0 else "scalar"
                wait(eng_w, now("sPE"))
                if s == 0:
                    wait("gpsimd", w2t_store[q4])
                gw2[q4 + 1] = w2t_load(q4 + 1, 0)
                gw2b[q4 + 1] = w2t_load(q4 + 1, 1)
            rsq_all[q4] = rsq_marks

        def emit_restack(q4):
            gwait("scalar", ("rpe",))
            for mm_ in rsq_all[q4]:
                wait("sync", mm_)
            for hh in range(2):
                sth = inc("sync", ["sRSa", "sRSb"][hh], 16, lambda e, hh=hh: e.dma_start(
                    out=bass.AP(drp.tensor, drp.offset + 64 * hh * 16384,
                                [[16384, 64], [1, 16384]]),
                    in_=rpn[64 * hh: 64 * hh + 64, :]))
                wait("scalar", sth)
                inc("scalar", "sRS", 16, lambda e, hh=hh: e.dma_start(
                    out=rpe[64 * hh: 64 * hh + 64, :].rearrange("p (g d) -> p g d", d=256),
                    in_=bass.AP(drp.tensor, drp.offset + 64 * hh * 16384,
                                [[256, 64], [16384, NG], [1, 256]])))
            rs_marks[q4] = now("sRS")
            gset(("rpn",), rs_marks[q4])

        def emit_scatter(q4):
            nonlocal sc_last
            wait("tensor", rs_marks[q4])
            for tt in range(NT):
                colo = 512 * (tt % 2)
                gwait("tensor", ("pw", tt % 2))
                inc("tensor", "sPE", 1, lambda e, tt=tt, q4=q4, colo=colo: e.matmul(
                    pp_w[:, colo: colo + 256], pmT[:, tt * 128: tt * 128 + 128],
                    c_b2[:, q4 * 256: q4 * 256 + 256], start=True, stop=False,
                    skip_group_check=True))
                scv = None
                for j in range(4):
                    g = tt * 4 + j
                    scv = inc("tensor", "sPE", 1, lambda e, tt=tt, j=j, g=g, colo=colo: e.matmul(
                        pp_w[j * 32: j * 32 + 32, colo: colo + 256],
                        pmT[:, tt * 128 + j * 32: tt * 128 + j * 32 + 32],
                        rpe[:, g * 256: g * 256 + 256],
                        start=False, stop=True,
                        tile_position=(0, 32 * j), skip_group_check=True))
                wait("vector", scv)
                m = inc("vector", "sDVE", 1, lambda e, tt=tt, q4=q4, colo=colo: e.tensor_tensor(
                    x2[:, tt * 1024 + q4 * 256: tt * 1024 + q4 * 256 + 256],
                    x2[:, tt * 1024 + q4 * 256: tt * 1024 + q4 * 256 + 256],
                    pp_w[:, colo: colo + 256], ALU.add))
                gset(("pw", tt % 2), m)
                sc_last = m
            gset(("rpe",), now("sPE"))

        emit_e2(0)
        emit_restack(0)
        emit_e2(1)
        emit_scatter(0)
        emit_restack(1)
        emit_e2(2)
        emit_scatter(1)
        emit_restack(2)
        emit_e2(3)
        emit_scatter(2)
        emit_restack(3)
        emit_scatter(3)
        wait("vector", sc_last)
        wait("vector", now("sDVE"))
        for tt in range(NT):
            inc("vector", "sDVE", 1, lambda e, tt=tt: e.tensor_reduce(
                st_m[:, tt: tt + 1], x2[:, tt * 1024: tt * 1024 + 1024], AX.X, ALU.add))
        if s == 0:
            wait("scalar", w2t_store[3])
        _ = emit_ln2_finish()
        st_marks = []
        for tt in range(NT):
            am = inc("scalar", "sACT", 1, lambda e, tt=tt: e.activation(
                out=x2[:, tt * 1024: tt * 1024 + 1024],
                in_=x2[:, tt * 1024: tt * 1024 + 1024],
                func=AF.Identity, scale=st_r[:, tt: tt + 1], bias=st_b[:, tt: tt + 1]))
            wait("sync", am)
            stm = inc("sync", "sST", 16, lambda e, s=s, tt=tt: e.dma_start(
                out=dy[s, tt * 128: tt * 128 + 128, :], in_=x2[:, tt * 1024: tt * 1024 + 1024]))
            st_marks.append(stm)
        prev_stores = st_marks[-1]
        if s == 0:
            st_all_ws = now("sWS")
            st_s2_all = now("sS2")
            st_s1a = now("sS1a"); st_s1b = now("sS1b")
        gset("R2xT", sc_last)
        gset("R4kv", now("sPE"))
        prev_relu = relu_done

    final_st = now("sST")
    wait("gpsimd", final_st)
    wait("gpsimd", now("sRS"))
    wait("sync", final_st)

    with nc.Block() as block:
        @block.sync
        def _(e):
            for fn in ops["sync"]: fn(e)
        @block.gpsimd
        def _(e):
            for fn in ops["gpsimd"]: fn(e)
        @block.tensor
        def _(e):
            for fn in ops["tensor"]: fn(e)
        @block.vector
        def _(e):
            for fn in ops["vector"]: fn(e)
        @block.scalar
        def _(e):
            for fn in ops["scalar"]: fn(e)
    for cm in cms:
        cm.__exit__(None, None, None)
    return nc


def host_constants(cfg):
    T = cfg.T
    pos = np.arange(T, dtype=np.float64)[:, None]
    inv_freq = np.exp(-math.log(10000.0) * np.arange(0, 128, 2, dtype=np.float64) / 128)
    ang = pos * inv_freq[None, :]
    pe = np.zeros((T, 128), dtype=np.float64)
    pe[:, 0::2] = np.sin(ang)
    pe[:, 1::2] = np.cos(ang)
    peT = pe.T.copy()
    pehi = peT.astype(np.float32)
    pelo = (peT - pehi.astype(np.float64)).astype(np.float32)
    tie = np.tile(np.linspace(0.0, 1e-6, 32, dtype=np.float64), T // 32).astype(BF)[None]
    ident = np.eye(128, dtype=BF)
    k = np.arange(128)[:, None]
    c = np.arange(896)[None, :]
    zl = (c >= k + 384).astype(BF)
    ones1 = np.ones((1, 128), dtype=BF)
    return dict(pehi=pehi, pelo=pelo, tie=tie, ident=ident,
                zl=zl, ones1=ones1)


def host_weights(cfg, W):
    D = cfg.D
    wq = np.ascontiguousarray(W["Wq"].reshape(8, 128, 1024).transpose(1, 0, 2)
                              ).reshape(128, 8 * 1024).astype(np.float32)
    wkv = np.ascontiguousarray(W["Wkv"].reshape(8, 128, 256).transpose(1, 0, 2)
                               ).reshape(128, 8 * 256).astype(np.float32)
    wo = np.ascontiguousarray(W["Wo"].reshape(8, 128, 1024).transpose(1, 0, 2)
                              ).reshape(128, 8 * 1024).astype(np.float32)
    wc = np.ascontiguousarray(W["Wc"].reshape(1024, 128).reshape(8, 128, 128)
                              .transpose(1, 0, 2)).reshape(128, 8 * 128).astype(np.float32)
    W1 = W["W1"].reshape(128, 1024, 32)
    w1 = W1.reshape(128, 4, 2, 128, 32).transpose(1, 3, 2, 0, 4)
    w1 = np.ascontiguousarray(w1).reshape(4, 128, 2 * 128 * 32).astype(np.float32)
    W2 = W["W2"].reshape(128, 1024, 32)
    w2 = np.zeros((4, 128, 32 * 256), dtype=np.float32)
    es = np.arange(128)
    b = es % 4; q = es // 4
    for q4 in range(4):
        blk = W2[:, q4 * 256:(q4 + 1) * 256, :]
        for e_ in range(128):
            w2[q4, b[e_] * 32:(b[e_] + 1) * 32,
               (q[e_] // 2) * 512 + (q[e_] % 2) * 256:
               (q[e_] // 2) * 512 + (q[e_] % 2) * 256 + 256] = blk[e_].T
    bq = np.ascontiguousarray(W["bq"].reshape(8, 128).T).astype(np.float32)
    bk = W["bkv"][:128].reshape(128, 1).astype(np.float32)
    bvb = np.broadcast_to(W["bkv"][128:], (128, 128)).copy().astype(np.float32)
    bc = W["bc"].reshape(128, 1).astype(np.float32)
    b1f = W["b1"].reshape(128, 32).astype(np.float32)
    b1 = np.zeros((128, 2048), dtype=np.float32)
    for es in range(128):
        b = es % 4; q = es // 4
        b1[b * 32:(b + 1) * 32, q * 64:(q + 1) * 64] = b1f[es][:, None]
    b2 = W["b2"].reshape(128, 1024).astype(np.float32)
    return dict(wq=wq, wkv=wkv, wo=wo, wc=wc, w1=w1, w2t=w2,
                bq=bq, bk=bk, bvb=bvb, bc=bc, b1=b1, b2=b2)



from concourse.bass_utils import run_bass_kernel_spmd

_CFG = Cfg(T=2048, NSEQ=4)
_NC_CACHE = [None]


def kernel(L0_attn_x, L0_attn_Wq, L0_attn_bq, L0_attn_Wkv, L0_attn_bkv, L0_attn_Wo,
           L0_ffn_Wc, L0_ffn_bc, L0_ffn_W1, L0_ffn_b1, L0_ffn_W2, L0_ffn_b2):
    cfg = _CFG
    x = np.ascontiguousarray(np.asarray(L0_attn_x, dtype=np.float32))
    B = x.shape[0]
    n_cores = 8
    per = B // n_cores
    W = dict(Wq=np.asarray(L0_attn_Wq), bq=np.asarray(L0_attn_bq),
             Wkv=np.asarray(L0_attn_Wkv), bkv=np.asarray(L0_attn_bkv),
             Wo=np.asarray(L0_attn_Wo), Wc=np.asarray(L0_ffn_Wc),
             bc=np.asarray(L0_ffn_bc), W1=np.asarray(L0_ffn_W1),
             b1=np.asarray(L0_ffn_b1), W2=np.asarray(L0_ffn_W2),
             b2=np.asarray(L0_ffn_b2))
    wl = host_weights(cfg, W)
    consts = host_constants(cfg)
    if _NC_CACHE[0] is None:
        _NC_CACHE[0] = build_nc(cfg)
    nc = _NC_CACHE[0]
    in_maps = []
    for c in range(n_cores):
        xs = x[c * per:(c + 1) * per]
        in_maps.append(dict(x=xs, xT=np.ascontiguousarray(xs.transpose(0, 2, 1)),
                            **wl, **consts))
    res = run_bass_kernel_spmd(nc, in_maps, core_ids=list(range(n_cores)))
    return np.concatenate([np.asarray(r["y"]) for r in res.results], axis=0)
